# version 17
# baseline (speedup 1.0000x reference)
import numpy as np
import ml_dtypes
import concourse.bass as bass
import concourse.mybir as mybir
from concourse.bass import IndirectOffsetOnAxis
from concourse.bass_utils import run_bass_kernel_spmd

F32 = mybir.dt.float32
BF16 = mybir.dt.bfloat16
FP8 = mybir.dt.float8e4
U32 = mybir.dt.uint32
I32 = mybir.dt.int32
AF = mybir.ActivationFunctionType
AX = mybir.AxisListType
ALU = mybir.AluOpType

H = 2048
HC = 16
S = 4096
STL = 32
TO = 1024
TT = 8
I = 5632
ICH = 44
IBL = 11
NMAC = 2
MTT = 4
QQ = 256
NQQ = 4
RSQH = 1.0 / float(np.sqrt(np.float32(2048.0)))
ENGS = ("sync", "scalar", "vector", "tensor", "gpsimd")
KBy = 1024


class KB:

    def __init__(self):
        self.ops = {e: [] for e in ENGS}
        self.cnt = {}
        self.sem_names = set()

    def count(self, sem):
        return self.cnt.get(sem, 0)

    def add(self, eng, fn, inc=None):
        if inc is not None:
            sem, n = inc
            self.sem_names.add(sem)
            self.cnt[sem] = self.cnt.get(sem, 0) + n

            def wrapped(e, sems, fn=fn, sem=sem, n=n):
                fn(e, sems).then_inc(sems[sem], n)
            self.ops[eng].append(wrapped)
        else:
            self.ops[eng].append(fn)

    def wait(self, eng, sem, val=None):
        self.sem_names.add(sem)
        if val is None:
            val = self.count(sem)
        if val <= 0:
            return

        def w(e, sems, sem=sem, val=val):
            e.wait_ge(sems[sem], val)
        self.ops[eng].append(w)

    def barrier(self):
        snap = dict(self.cnt)
        for eng in ENGS:
            for sem, val in snap.items():
                self.wait(eng, sem, val)

    def replay(self, eng_name, e, sems):
        for fn in self.ops[eng_name]:
            fn(e, sems)


def build_program():
    nc = bass.Bass("TRN2", target_bir_lowering=False)

    xb = nc.dram_tensor("xb", [S, H], F32, kind="ExternalInput").ap()
    xo = nc.dram_tensor("xo", [TO, H], F32, kind="ExternalInput").ap()
    wq = nc.dram_tensor("wq", [H, H], F32, kind="ExternalInput").ap()
    wk = nc.dram_tensor("wk", [H, H], F32, kind="ExternalInput").ap()
    wvt = nc.dram_tensor("wvt", [H, H], F32, kind="ExternalInput").ap()
    wot = nc.dram_tensor("wot", [H, H], F32, kind="ExternalInput").ap()
    bqv = nc.dram_tensor("bqv", [H], F32, kind="ExternalInput").ap()
    bvv = nc.dram_tensor("bvv", [H], F32, kind="ExternalInput").ap()
    bov = nc.dram_tensor("bov", [H], F32, kind="ExternalInput").ap()
    wg = nc.dram_tensor("wg", [I, H], F32, kind="ExternalInput").ap()
    wu = nc.dram_tensor("wu", [I, H], F32, kind="ExternalInput").ap()
    wd = nc.dram_tensor("wd", [H, I], F32, kind="ExternalInput").ap()
    pidx = nc.dram_tensor("pidx", [I], I32, kind="ExternalInput").ap()
    out = nc.dram_tensor("out", [TO, H], F32, kind="ExternalOutput").ap()

    d_xnb = nc.dram_tensor("d_xnb", [S, H], BF16).ap()
    d_wqb = nc.dram_tensor("d_wqb", [H, H], BF16).ap()
    d_qg = nc.dram_tensor("d_qg", [I, H], BF16).ap()
    d_qu = nc.dram_tensor("d_qu", [I, H], BF16).ap()
    d_wdq = nc.dram_tensor("d_wdq", [H, I], BF16).ap()
    d_wdt = nc.dram_tensor("d_wdt", [I, H], BF16).ap()
    d_qt = nc.dram_tensor("d_qt", [512, I], BF16).ap()
    d_z = nc.dram_tensor("d_z", [HC, 128, TO], BF16).ap()
    d_gf = nc.dram_tensor("d_gf", [TT, 128, H], BF16).ap()
    d_dbf = nc.dram_tensor("d_dbf", [1, S], BF16).ap()
    d_xnf = nc.dram_tensor("d_xnf", [8, 128, HC * 512], BF16).ap()
    d_wsg = nc.dram_tensor("d_wsg", [1, I], F32).ap()
    d_wsu = nc.dram_tensor("d_wsu", [1, I], F32).ap()
    d_wsd = nc.dram_tensor("d_wsd", [1, H], F32).ap()

    K = KB()
    from contextlib import ExitStack
    stack = ExitStack()
    with stack:
        AR = 200 * KBy
        arena = stack.enter_context(nc.sbuf_tensor("arena", [128, AR], mybir.dt.uint8))

        def view(off, dtype, shape):
            n = int(np.prod(shape[1:]))
            nb = n * mybir.dt.size(dtype)
            assert off % 64 == 0 and off + nb <= AR, (off, nb)
            v = arena[:, off:off + nb].bitcast(dtype)
            if len(shape) == 2:
                return v
            return v.rearrange("p (m q) -> p m q", q=shape[2])

        ones_bf = view(0, BF16, [128, 128])[0:1, :]
        pcols = view(256, U32, [128, ICH])
        bq_cols = view(448, BF16, [128, HC])
        bv_cols = view(512, F32, [128, HC])
        wbar_cols = view(576, BF16, [128, HC])
        sx_cols = view(640, F32, [128, TT])
        st_cols = view(704, F32, [128, TT])
        smalls = view(768, F32, [128, 64])
        (SUM, SSQ, MEAN, RSTD, VARC, RMX, RCP, C2, MTMP, MCOL0, MCOL1, NEGM0,
         NEGM1, LCOL0, LCOL1, LINV0, LINV1, TMPA, TMPB,
         RMX1, RCP1, C21, TMPA1, SUM1, SSQ1, MEAN1, RSTD1, VARC1, TMPB1,
         RMX2, RCP2, C22, TMPA2, RMX3, RCP3, C23, TMPA3) = range(37)

        def col(i):
            return smalls[:, i:i + 1]

        B = 2 * KBy

        wk_bf = view(B, BF16, [128, HC, H])
        A_sb = view(B + 64 * KBy, BF16, [128, HC, H])
        wq_cb = [view(B + 130 * KBy + i * 4 * KBy, BF16, [128, HC, 128]) for i in range(2)]

        xt_bf = [view(B + 138 * KBy + i * 4 * KBy, BF16, [128, H]) for i in range(2)]
        sq_scr = view(B + 146 * KBy, BF16, [128, H])
        disc_bf = view(B + 150 * KBy, BF16, [128, H])
        xt_f32 = [view(B + 154 * KBy + i * 8 * KBy, F32, [128, H]) for i in range(2)]
        wk_f32 = [view(B + 170 * KBy + i * 8 * KBy, F32, [128, H]) for i in range(2)]

        xo_f32 = [view(B + i * 8 * KBy, F32, [128, H]) for i in range(2)]
        xn_scr = view(B + 16 * KBy, F32, [128, H])
        sq3_scr = view(B + 24 * KBy, BF16, [128, H])
        qx8_tm = view(B + 28 * KBy, FP8, [128, H])
        qx_bf_tm = view(B + 30 * KBy, BF16, [128, H])
        tb_scr = view(B + 34 * KBy, BF16, [128, HC, 128])
        z_scr = [view(B + 38 * KBy + i * KBy, BF16, [128, 512]) for i in range(2)]
        disc3 = view(B + 40 * KBy, BF16, [128, H])
        xn_own_fm = view(B + 128 * KBy, BF16, [128, HC, TO])
        qx_fm8 = view(B + 160 * KBy, FP8, [128, HC, TO])

        wt_f32 = ([view(B + 44 * KBy + i * 8 * KBy, F32, [128, H]) for i in range(2)]
                  + [view(B + 176 * KBy + i * 8 * KBy, F32, [128, H]) for i in range(2)])
        q8_scr = ([view(B + 60 * KBy + i * 2 * KBy, FP8, [128, H]) for i in range(2)]
                  + [view(B + 192 * KBy + i * 2 * KBy, FP8, [128, H]) for i in range(2)])
        wdp_f32 = [view(B, F32, [128, I]), view(B + 176 * KBy, F32, [128, I])]
        q8_big = [view(B + 22 * KBy + i * 6 * KBy, FP8, [128, I]) for i in range(2)]
        wdt_scr = [view(B + 152 * KBy + i * 4 * KBy, BF16, [128, H]) for i in range(2)]

        t2v = view(B, F32, [128, MTT, I])
        g_rhsb = [view(B + 88 * KBy + i * 16 * KBy, BF16, [128, HC, 512]) for i in range(2)]
        u_rhsb = [view(B + 120 * KBy + i * 16 * KBy, BF16, [128, HC, 512]) for i in range(2)]
        qt_fm = view(B + 88 * KBy, BF16, [128, ICH, 512])
        qt8_tm = view(B + 152 * KBy, FP8, [128, I])
        wd_rhs = [view(B + 188 * KBy + i * 2 * KBy, BF16, [128, 1024]) for i in range(2)]
        wsg_bc = [view(B + 176 * KBy + i * 2 * KBy, F32, [128, 512]) for i in range(2)]
        wsu_bc = [view(B + 180 * KBy + i * 2 * KBy, F32, [128, 512]) for i in range(2)]
        wsd_bc = [view(B + 184 * KBy + i * 2 * KBy, F32, [128, 512]) for i in range(2)]
        g1_scr = view(B + 188 * KBy, F32, [128, 512])
        sg_scr = view(B + 190 * KBy, F32, [128, 512])
        u1_scr = view(B + 192 * KBy, F32, [128, 512])
        y1_scr = view(B + 194 * KBy, F32, [128, 512])
        gf_scr = view(B + 196 * KBy, BF16, [128, 512])

        scores = [view(B + i * 16 * KBy, F32, [128, S]) for i in range(2)]
        p_bf = [view(B + 32 * KBy + i * 8 * KBy, BF16, [128, S]) for i in range(2)]
        pT = view(B + 48 * KBy, BF16, [128, 32, QQ])
        fm_dest = [view(B + 64 * KBy + i * 16 * KBy, BF16, [128, HC, 512]) for i in range(2)]
        px_fm = view(B + 96 * KBy, BF16, [128, HC, TO])
        ctx_fm = view(B + 128 * KBy, BF16, [128, HC, TO])
        px_scr = view(B + 160 * KBy, BF16, [128, H])
        z_sb = view(B + 164 * KBy, BF16, [128, HC, QQ])
        xnb_tile = [view(B + 172 * KBy + i * 4 * KBy, BF16, [128, H]) for i in range(4)]
        d_row_bf = view(B + 188 * KBy, BF16, [1, 512])
        d_full = view(B + 189 * KBy, BF16, [1, S])
        wbig = view(B, BF16, [128, HC, H])
        bo_bc = view(B + 184 * KBy, F32, [128, 512])
        s1_big = view(B + 64 * KBy, F32, [128, TT, 512])
        gf_tile = [view(B + 80 * KBy + i * KBy, BF16, [128, 512]) for i in range(4)]
        xo_blk = [view(B + 84 * KBy + i * 2 * KBy, F32, [128, 512]) for i in range(4)]
        out_scr = [view(B + 92 * KBy + i * 2 * KBy, F32, [128, 512]) for i in range(2)]

        PB = [stack.enter_context(nc.psum_tensor("pb%d" % i, [128, 512], F32))
              for i in range(8)]
        def dma(eng, sem, out_ap, in_ap, **kw):
            K.add(eng, lambda e, s, o=out_ap, i=in_ap, kw=kw:
                  e.dma_start(out=o, in_=i, **kw), inc=(sem, 16))

        def dmat(sem, out_ap, in_ap, eng="sync"):
            K.add(eng, lambda e, s, o=out_ap, i=in_ap:
                  e.dma_start_transpose(out=o, in_=i), inc=(sem, 16))

        def mm(lhsT, rhs, pout, start, stop, inc=False):
            K.add("tensor",
                  lambda e, s, l=lhsT, r=rhs, p=pout, st=start, sp=stop:
                  e.matmul(p, l, r, start=st, stop=sp),
                  inc=("sT", 1) if inc else None)

        def vop(fn, inc=False):
            K.wait("vector", "sV")
            K.add("vector", fn, inc=("sV", 1))

        def aop(fn, inc=False):
            K.wait("scalar", "sC")
            K.add("scalar", fn, inc=("sC", 1))

        def gop(sem, fn):
            K.add("gpsimd", fn, inc=(sem, 16))

        def dram_col(dram_row_ap):
            return dram_row_ap.rearrange("o (p q) -> p (o q)", q=1)

        vop(lambda e, s: e.memset(ones_bf, 1.0))
        for c in range(HC):
            dma("gpsimd", "qCW", d_wqb[c * 128:(c + 1) * 128, :],
                wq[c * 128:(c + 1) * 128, :])
        for j in range(ICH):
            dma("gpsimd", "qPC", pcols[:, j:j + 1],
                pidx[j * 128:(j + 1) * 128].rearrange("(p o) -> p o", o=1).bitcast(U32))
        for c in range(HC):
            dma("gpsimd", "qPC", bq_cols[:, c:c + 1],
                bqv[c * 128:(c + 1) * 128].rearrange("(p o) -> p o", o=1))
            dma("gpsimd", "qPC", bv_cols[:, c:c + 1],
                bvv[c * 128:(c + 1) * 128].rearrange("(p o) -> p o", o=1))

        def gen_p1():
            snapw = {}
            for c in range(HC):
                par = c % 2
                if par in snapw:
                    K.wait("sync", "sV", snapw[par])
                dma("sync", "qWK%d" % par, wk_f32[par], wk[c * 128:(c + 1) * 128, :])
                K.wait("vector", "qWK%d" % par)
                vop(lambda e, s, c=c, par=par:
                    e.tensor_copy(wk_bf[:, c, :], wk_f32[par][:, :]), inc=True)
                snapw[par] = K.count("sV")
            wk_done = K.count("sV")
            K.wait("sync", "qCW")
            bank = 0
            for hi in range(HC):
                cb = wq_cb[hi % 2]
                qn = "qWQ%d" % (hi % 2)
                if hi >= 2:
                    K.wait("sync", "sT")
                for o in range(HC):
                    dma("sync", qn, cb[:, o, :],
                        d_wqb[o * 128:(o + 1) * 128, hi * 128:(hi + 1) * 128])
                K.wait("tensor", qn)
                K.wait("tensor", "sV", wk_done)
                for hob in range(4):
                    pb = PB[bank % 8]
                    if bank >= 8:
                        K.wait("tensor", "sV")
                    for o in range(HC):
                        mm(cb[:, o, :], wk_bf[:, o, hob * 512:(hob + 1) * 512],
                           pb[:, :], start=(o == 0), stop=(o == HC - 1),
                           inc=(o == HC - 1))
                    K.wait("vector", "sT")
                    vop(lambda e, s, pb=pb, hi=hi, hob=hob:
                        e.tensor_copy(A_sb[:, hi, hob * 512:(hob + 1) * 512], pb[:, :]),
                        inc=True)
                    bank += 1
            K.wait("tensor", "qPC")
            for hob in range(4):
                pb = PB[bank % 8]
                K.wait("tensor", "sV")
                for o in range(HC):
                    mm(bq_cols[:, o:o + 1], wk_bf[:, o, hob * 512:(hob + 1) * 512],
                       pb[0:1, :], start=(o == 0), stop=(o == HC - 1), inc=(o == HC - 1))
                K.wait("vector", "sT")
                vop(lambda e, s, pb=pb: e.tensor_scalar(
                    d_row_bf[0:1, :], pb[0:1, :], RSQH, None, ALU.mult), inc=True)
                K.wait("gpsimd", "sV")
                dma("gpsimd", "qDBF", d_dbf[0:1, hob * 512:(hob + 1) * 512], d_row_bf[0:1, :])
                K.wait("vector", "qDBF")
                bank += 1
                yield
            K.wait("gpsimd", "qDBF")
            for c in range(HC):
                dma("gpsimd", "qPC", wbar_cols[:, c:c + 1],
                    dram_col(d_dbf[0:1, c * 128:(c + 1) * 128]))

        def gen_p2():
            snap2 = {}
            for t in range(STL):
                par = t % 2
                buf = xt_f32[par]
                obuf = xt_bf[par]
                qn = "qX2%d" % par
                cS, cQ, cM, cV, cR, cT = ((SUM, SSQ, MEAN, VARC, RSTD, TMPB) if par == 0
                                          else (SUM1, SSQ1, MEAN1, VARC1, RSTD1, TMPB1))
                cA = TMPA if par == 0 else TMPA1
                if t >= 2:
                    for sem, val in snap2[par]:
                        K.wait("gpsimd", sem, val)
                dma("gpsimd", qn, buf, xb[t * 128:(t + 1) * 128, :])
                K.wait("scalar", qn)
                aop(lambda e, s, b=buf, cQ=cQ: e.activation(sq_scr[:, :], b, AF.Square,
                                                            accum_out=col(cQ)))
                K.wait("vector", qn)
                vop(lambda e, s, b=buf, cS=cS, par=par: e.tensor_scalar(
                    disc_bf[:, :], b, 1.0, 0.0, ALU.mult, ALU.add, accum_out=col(cS)))
                K.wait("vector", "sC")
                vop(lambda e, s, cM=cM, cS=cS: e.tensor_scalar(col(cM), col(cS), 1.0 / H, None, ALU.mult))
                vop(lambda e, s, cV=cV, cQ=cQ: e.tensor_scalar(col(cV), col(cQ), 1.0 / H, None, ALU.mult))
                vop(lambda e, s, cA=cA, cM=cM: e.tensor_tensor(col(cA), col(cM), col(cM), ALU.mult))
                vop(lambda e, s, cV=cV, cA=cA: e.tensor_tensor(col(cV), col(cV), col(cA), ALU.subtract))
                vop(lambda e, s, cV=cV: e.tensor_scalar(col(cV), col(cV), 1e-5, None, ALU.add),
                    inc=True)
                K.wait("scalar", "sV")
                aop(lambda e, s, cT=cT, cV=cV: e.activation(col(cT), col(cV), AF.Sqrt))
                K.wait("vector", "sC")
                vop(lambda e, s, cR=cR, cT=cT: e.reciprocal(col(cR), col(cT)))
                vop(lambda e, s, b=buf, ob=obuf, cM=cM, cR=cR: e.tensor_scalar(
                    ob, b, col(cM), col(cR), ALU.subtract, ALU.mult), inc=True)
                K.wait("scalar", "sV")
                dma("scalar", "qNB%d" % par, d_xnb[t * 128:(t + 1) * 128, :], obuf)
                snap2[par] = [("sV", K.count("sV")), ("sC", K.count("sC")),
                              ("qNB%d" % par, K.count("qNB%d" % par))]
                yield

        def gen_p3():
            snap3 = {}
            for t in range(TT):
                par = t % 2
                buf = xo_f32[par]
                qn = "qX3%d" % par
                cS, cQ, cM, cV, cR, cT = ((SUM, SSQ, MEAN, VARC, RSTD, TMPB) if par == 0
                                          else (SUM1, SSQ1, MEAN1, VARC1, RSTD1, TMPB1))
                cA = TMPA if par == 0 else TMPA1
                rmx, rcp, c2 = (RMX, RCP, C2) if par == 0 else (RMX1, RCP1, C21)
                if t >= 2:
                    for sem, val in snap3[par]:
                        K.wait("gpsimd", sem, val)
                dma("gpsimd", qn, buf, xo[t * 128:(t + 1) * 128, :])
                K.wait("scalar", qn)
                aop(lambda e, s, b=buf, cQ=cQ: e.activation(sq3_scr[:, :], b, AF.Square,
                                                            accum_out=col(cQ)))
                K.wait("vector", qn)
                vop(lambda e, s, b=buf, cS=cS: e.tensor_scalar(
                    disc3[:, :], b, 1.0, 0.0, ALU.mult, ALU.add, accum_out=col(cS)))
                K.wait("vector", "sC")
                vop(lambda e, s, cM=cM, cS=cS: e.tensor_scalar(col(cM), col(cS), 1.0 / H, None, ALU.mult))
                vop(lambda e, s, cV=cV, cQ=cQ: e.tensor_scalar(col(cV), col(cQ), 1.0 / H, None, ALU.mult))
                vop(lambda e, s, cA=cA, cM=cM: e.tensor_tensor(col(cA), col(cM), col(cM), ALU.mult))
                vop(lambda e, s, cV=cV, cA=cA: e.tensor_tensor(col(cV), col(cV), col(cA), ALU.subtract))
                vop(lambda e, s, cV=cV: e.tensor_scalar(col(cV), col(cV), 1e-5, None, ALU.add),
                    inc=True)
                K.wait("scalar", "sV")
                aop(lambda e, s, cT=cT, cV=cV: e.activation(col(cT), col(cV), AF.Sqrt))
                K.wait("vector", "sC")
                vop(lambda e, s, cR=cR, cT=cT: e.reciprocal(col(cR), col(cT)))
                K.wait("vector", "qFM")
                K.wait("vector", "qTB")
                K.wait("vector", "sC")
                vop(lambda e, s, b=buf, cM=cM, cR=cR: e.tensor_scalar(
                    xn_scr[:, :], b, col(cM), col(cR), ALU.subtract, ALU.mult))
                vop(lambda e, s: e.tensor_copy(qx_bf_tm[:, :], xn_scr[:, :]))
                vop(lambda e, s, rmx=rmx: e.reduce_max(col(rmx), xn_scr[:, :], AX.X,
                                                       apply_absolute_value=True))
                vop(lambda e, s, rmx=rmx, rcp=rcp: e.reciprocal(col(rcp), col(rmx)))
                vop(lambda e, s, rcp=rcp, c2=c2: e.tensor_scalar(col(c2), col(rcp), 224.0,
                                                                 None, ALU.mult))
                vop(lambda e, s, t=t, rmx=rmx: e.tensor_scalar(
                    sx_cols[:, t:t + 1], col(rmx), 1.0 / 224.0, None, ALU.mult), inc=True)
                K.wait("sync", "sV")
                dmat("qFM", xn_own_fm[:, :, t * 128:(t + 1) * 128], qx_bf_tm[:, :])
                K.wait("scalar", "sV")
                aop(lambda e, s, c2=c2: e.activation(qx8_tm[:, :], xn_scr[:, :], AF.Copy,
                                                     scale=col(c2)))
                K.wait("vector", "sC")
                K.wait("vector", "qFM")
                vop(lambda e, s: e.tensor_copy(qx_bf_tm[:, :], qx8_tm[:, :]), inc=True)
                K.wait("sync", "sV")
                K.wait("sync", "sC")
                dmat("qTB", tb_scr[:, :, :], qx_bf_tm[:, :])
                K.wait("scalar", "qTB")
                aop(lambda e, s, t=t: e.activation(
                    qx_fm8[:, :, t * 128:(t + 1) * 128], tb_scr[:, :, :], AF.Copy))
                snap3[par] = [("sV", K.count("sV")), ("sC", K.count("sC"))]
            yield

        def gen_p35():
            snapz = {}
            bank = 0
            for ho in range(HC):
                for qb in range(2):
                    pb = PB[bank % 8]
                    par = bank % 2
                    if bank >= 8:
                        K.wait("tensor", "sC")
                    for hi in range(HC):
                        mm(A_sb[:, hi, ho * 128:(ho + 1) * 128],
                           xn_own_fm[:, hi, qb * 512:(qb + 1) * 512],
                           pb[:, :], start=(hi == 0), stop=(hi == HC - 1),
                           inc=(hi == HC - 1))
                    K.wait("scalar", "sT")
                    if par in snapz:
                        for sem, val in snapz[par]:
                            K.wait("scalar", sem, val)
                    aop(lambda e, s, pb=pb, par=par: e.activation(
                        z_scr[par][:, :], pb[:, :], AF.Copy, scale=RSQH))
                    K.wait("gpsimd", "sC")
                    dma("gpsimd", "qZS%d" % par, d_z[ho, :, qb * 512:(qb + 1) * 512],
                        z_scr[par][:, :])
                    snapz[par] = [("qZS%d" % par, K.count("qZS%d" % par))]
                    bank += 1
                    yield

        def gen_p4a():
            cols = [(RMX, RCP, C2, TMPA), (RMX1, RCP1, C21, TMPA1),
                    (RMX2, RCP2, C22, TMPA2), (RMX3, RCP3, C23, TMPA3)]
            snapa = {}
            for wi, (wsrc, qdst, wsdst) in enumerate(((wg, d_qg, d_wsg), (wu, d_qu, d_wsu))):
                for j in range(ICH):
                    jj = wi * ICH + j
                    par = jj % 4
                    buf = wt_f32[par]
                    qn = "qW4%d" % par
                    rmx, rcp, c2, tmpa = cols[par]
                    if par in snapa:
                        for sem, val in snapa[par]:
                            K.wait("sync", sem, val)
                    dma("sync", qn, buf, wsrc[j * 128:(j + 1) * 128, :])
                    K.wait("vector", qn)
                    vop(lambda e, s, b=buf, rmx=rmx: e.reduce_max(
                        col(rmx), b, AX.X, apply_absolute_value=True))
                    vop(lambda e, s, rmx=rmx, rcp=rcp: e.reciprocal(col(rcp), col(rmx)))
                    vop(lambda e, s, rcp=rcp, c2=c2: e.tensor_scalar(
                        col(c2), col(rcp), 224.0, None, ALU.mult))
                    vop(lambda e, s, rmx=rmx, tmpa=tmpa: e.tensor_scalar(
                        col(tmpa), col(rmx), 1.0 / 224.0, None, ALU.mult), inc=True)
                    K.wait("gpsimd", "sV")
                    dma("gpsimd", "qWSA%d" % par, dram_col(wsdst[0:1, j * 128:(j + 1) * 128]),
                        col(tmpa))
                    K.wait("scalar", "sV")
                    K.wait("scalar", "qQG%d" % par)
                    aop(lambda e, s, b=buf, c2=c2, par=par: e.activation(
                        q8_scr[par][:, :], b, AF.Copy, scale=col(c2)))
                    K.wait("gpsimd", "sC")
                    dma("gpsimd", "qQG%d" % par, qdst[j * 128:(j + 1) * 128, :],
                        q8_scr[par][:, :])
                    snapa[par] = [("sV", K.count("sV")), ("sC", K.count("sC")),
                                  ("qQG%d" % par, K.count("qQG%d" % par)),
                                  ("qWSA%d" % par, K.count("qWSA%d" % par))]
                    yield
        def gen_p4b():
            snapb = {}
            cols = [(RMX, RCP, C2, TMPA), (RMX1, RCP1, C21, TMPA1)]
            for j in range(HC):
                par = j % 2
                buf = wdp_f32[par]
                qn = "qW5%d" % par
                rmx, rcp, c2, tmpa = cols[par]
                if par in snapb:
                    for sem, val in snapb[par]:
                        K.wait("sync", sem, val)
                dma("sync", qn, buf, wd[j * 128:(j + 1) * 128, :])
                K.wait("vector", qn)
                vop(lambda e, s, b=buf, rmx=rmx: e.reduce_max(
                    col(rmx), b, AX.X, apply_absolute_value=True))
                vop(lambda e, s, rmx=rmx, rcp=rcp: e.reciprocal(col(rcp), col(rmx)))
                vop(lambda e, s, rcp=rcp, c2=c2: e.tensor_scalar(
                    col(c2), col(rcp), 224.0, None, ALU.mult))
                vop(lambda e, s, rmx=rmx, tmpa=tmpa: e.tensor_scalar(
                    col(tmpa), col(rmx), 1.0 / 224.0, None, ALU.mult), inc=True)
                K.wait("gpsimd", "sV")
                dma("gpsimd", "qWSA%d" % par, dram_col(d_wsd[0:1, j * 128:(j + 1) * 128]),
                    col(tmpa))
                K.wait("scalar", "sV")
                K.wait("scalar", "qQG%d" % par)
                aop(lambda e, s, b=buf, c2=c2, par=par: e.activation(
                    q8_big[par][:, :], b, AF.Copy, scale=col(c2)))
                K.wait("gpsimd", "sC")
                dma("gpsimd", "qQG%d" % par, d_wdq[j * 128:(j + 1) * 128, :],
                    q8_big[par][:, :])
                snapb[par] = [("sV", K.count("sV")), ("sC", K.count("sC")),
                              ("qQG%d" % par, K.count("qQG%d" % par)),
                              ("qWSA%d" % par, K.count("qWSA%d" % par))]
                yield
        def gen_p4c():
            for j in range(ICH):
                buf = wdt_scr[j % 2]
                qn = "qT4%d" % (j % 2)
                if j >= 2:
                    K.wait("sync", "qSC")
                dmat(qn, buf, d_wdq[:, j * 128:(j + 1) * 128])
                K.wait("gpsimd", qn)
                gop("qSC", lambda e, s, b=buf, j=j: e.indirect_dma_start(
                    out=d_wdt, out_offset=IndirectOffsetOnAxis(ap=pcols[:, j:j + 1], axis=0),
                    in_=b, in_offset=None))
                yield

        def gen_p4d():
            snapg = {}
            for mac in range(NMAC):
                tts = [mac * MTT + i for i in range(MTT)]
                for ib in range(IBL):
                    gib = mac * IBL + ib
                    par = gib % 2
                    K.wait("gpsimd", "sV")
                    dma("gpsimd", "qWS", wsg_bc[ib % 2],
                        d_wsg[0:1, ib * 512:(ib + 1) * 512].broadcast_to([128, 512]))
                    dma("gpsimd", "qWS", wsu_bc[ib % 2],
                        d_wsu[0:1, ib * 512:(ib + 1) * 512].broadcast_to([128, 512]))
                    qn = "qGR%d" % par
                    if ib == 0 and mac > 0:
                        K.wait("sync", "sT")
                    if par in snapg:
                        for sem, val in snapg[par]:
                            K.wait("sync", sem, val)
                    dmat(qn, g_rhsb[par][:, :, :], d_qg[ib * 512:(ib + 1) * 512, :])
                    dmat(qn, u_rhsb[par][:, :, :], d_qu[ib * 512:(ib + 1) * 512, :])
                    K.wait("tensor", qn)
                    for half in range(2):
                        hts = tts[half * 2:(half + 1) * 2]
                        hkey = "gh%d" % half
                        pbase = half * 4
                        if hkey in snapg:
                            K.wait("tensor", "sV", snapg[hkey])
                        for hc in range(HC):
                            for i_h, t in enumerate(hts):
                                mm(qx_fm8[:, hc, t * 128:(t + 1) * 128], g_rhsb[par][:, hc, :],
                                   PB[pbase + i_h][:, :], start=(hc == 0), stop=(hc == HC - 1))
                                mm(qx_fm8[:, hc, t * 128:(t + 1) * 128], u_rhsb[par][:, hc, :],
                                   PB[pbase + 2 + i_h][:, :], start=(hc == 0),
                                   stop=(hc == HC - 1),
                                   inc=(hc == HC - 1 and i_h == 1))
                        snapg[par] = [("sT", K.count("sT"))]
                        K.wait("vector", "sT")
                        K.wait("vector", "qWS")
                        for i_h, t in enumerate(hts):
                            vop(lambda e, s, pbase=pbase, i_h=i_h, t=t, ib=ib:
                                e.scalar_tensor_tensor(
                                    g1_scr[:, :], PB[pbase + i_h][:, :], sx_cols[:, t:t + 1],
                                    wsg_bc[ib % 2], ALU.mult, ALU.mult), inc=True)
                            K.wait("scalar", "sV")
                            aop(lambda e, s: e.activation(sg_scr[:, :], g1_scr[:, :], AF.Silu))
                            vop(lambda e, s, pbase=pbase, i_h=i_h, t=t, ib=ib:
                                e.scalar_tensor_tensor(
                                    u1_scr[:, :], PB[pbase + 2 + i_h][:, :],
                                    sx_cols[:, t:t + 1], wsu_bc[ib % 2],
                                    ALU.mult, ALU.mult))
                            K.wait("vector", "sC")
                            it_g = half * 2 + i_h
                            vop(lambda e, s, it_g=it_g, ib=ib: e.tensor_tensor(
                                t2v[:, it_g, ib * 512:(ib + 1) * 512], sg_scr[:, :],
                                u1_scr[:, :], ALU.mult), inc=True)
                        snapg[hkey] = K.count("sV")
                        yield
                for qi in range(4):
                    K.wait("gpsimd", "qQTF%d" % qi)
                for i_t, t in enumerate(tts):
                    vop(lambda e, s, i_t=i_t: e.reduce_max(
                        col(RMX), t2v[:, i_t, :], AX.X, apply_absolute_value=True), inc=True)
                    K.wait("vector", "sC")
                    vop(lambda e, s: e.reciprocal(col(RCP), col(RMX)))
                    vop(lambda e, s: e.tensor_scalar(col(C2), col(RCP), 224.0, None, ALU.mult))
                    vop(lambda e, s, t=t: e.tensor_scalar(
                        st_cols[:, t:t + 1], col(RMX), 1.0 / 224.0, None, ALU.mult), inc=True)
                    K.wait("scalar", "sV")
                    K.wait("scalar", "qQT")
                    K.wait("scalar", "qSC")
                    aop(lambda e, s, i_t=i_t: e.activation(qt8_tm[:, :], t2v[:, i_t, :],
                                                           AF.Copy, scale=col(C2)), inc=True)
                    K.wait("gpsimd", "sC")
                    dma("gpsimd", "qQT", d_qt[i_t * 128:(i_t + 1) * 128, :], qt8_tm[:, :])
                    yield
                K.wait("sync", "qQT")
                K.wait("sync", "sT")
                for ic in range(ICH):
                    dmat("qQTF%d" % (ic % 4), qt_fm[:, ic, :],
                         d_qt[:, ic * 128:(ic + 1) * 128])
                K.wait("sync", "qSC")
                K.wait("scalar", "qSC")
                K.wait("sync", "sV")
                qtf_snap = [K.count("qQTF%d" % (ic % 4)) for ic in range(ICH)]
                for hp in range(2):
                    hbs = (2 * hp, 2 * hp + 1)
                    K.wait("gpsimd", "sV")
                    for bi, hb in enumerate(hbs):
                        dma("gpsimd", "qWS", wsd_bc[bi],
                            d_wsd[0:1, hb * 512:(hb + 1) * 512].broadcast_to([128, 512]))
                    if hp == 1 or mac > 0:
                        K.wait("tensor", "sV")
                    for ic in range(ICH):
                        buf = wd_rhs[ic % 2]
                        qn = "qWD%d" % (ic % 2)
                        if ic >= 2 or hp == 1 or mac > 0:
                            K.wait("sync", "sT")
                        dma("sync", qn, buf,
                            d_wdt[ic * 128:(ic + 1) * 128,
                                  2 * hp * 512:(2 * hp + 2) * 512])
                        K.wait("tensor", qn)
                        K.wait("tensor", "qQTF%d" % (ic % 4), qtf_snap[ic])
                        for bi, hb in enumerate(hbs):
                            for i_t, t in enumerate(tts):
                                mm(qt_fm[:, ic, i_t * 128:(i_t + 1) * 128],
                                   buf[:, bi * 512:(bi + 1) * 512],
                                   PB[bi * 4 + i_t][:, :],
                                   start=(ic == 0), stop=(ic == ICH - 1),
                                   inc=(bi == 1 and i_t == MTT - 1))
                    K.wait("vector", "sT")
                    K.wait("vector", "qWS")
                    for bi, hb in enumerate(hbs):
                        for i_t, t in enumerate(tts):
                            vop(lambda e, s, bi=bi, i_t=i_t, t=t: e.scalar_tensor_tensor(
                                y1_scr[:, :], PB[bi * 4 + i_t][:, :], st_cols[:, t:t + 1],
                                wsd_bc[bi], ALU.mult, ALU.mult), inc=True)
                            K.wait("scalar", "sV")
                            K.wait("scalar", "qGF")
                            aop(lambda e, s: e.activation(gf_scr[:, :], y1_scr[:, :],
                                                          AF.Silu))
                            K.wait("gpsimd", "sC")
                            dma("gpsimd", "qGF", d_gf[t, :, hb * 512:(hb + 1) * 512],
                                gf_scr[:, :])
                            K.wait("vector", "sC")
                    yield
                    snapg["gh0"] = K.count("sV")
                    snapg["gh1"] = K.count("sV")
                    yield


        def ilv(*gens):
            gs = [(g, 1) if not isinstance(g, tuple) else g for g in gens]
            gs = [[g, w] for g, w in gs]
            while gs:
                for item in list(gs):
                    g, w = item
                    for _ in range(w):
                        try:
                            next(g)
                        except StopIteration:
                            gs.remove(item)
                            break

        ilv(gen_p1(), gen_p2())
        K.barrier()
        ilv(gen_p3(), gen_p4a())
        K.barrier()
        ilv(gen_p35(), gen_p4b())
        K.barrier()
        ilv(gen_p4d(), (gen_p4c(), 4))
        K.barrier()
        snap5 = {}
        snapx = {}
        for qq in range(NQQ):
            K.wait("scalar", "sT")
            K.wait("scalar", "sC")
            for ho in range(HC):
                dma("scalar", "qZL", z_sb[:, ho, :], d_z[ho, :, qq * QQ:(qq + 1) * QQ])
            if qq > 0:
                K.wait("scalar", "qDBF")
                dma("scalar", "qDR", d_full[0:1, :], d_dbf[0:1, :])
            vop(lambda e, s: e.memset(col(MCOL0), -1e30))
            vop(lambda e, s: e.memset(col(MCOL1), -1e30))
            K.wait("vector", "sC")
            K.wait("tensor", "qZL")
            K.wait("tensor", "qDR")
            K.wait("vector", "qZL")
            for kb in range(8):
                par = kb % 2
                fmb = fm_dest[par]
                qn = "qFM%d" % par
                if par in snap5:
                    for sem, val in snap5[par]:
                        K.wait("sync", sem, val)
                if qq == 0:
                    dmat(qn, fmb, d_xnb[kb * 512:(kb + 1) * 512, :])
                    K.wait("gpsimd", qn)
                    dma("gpsimd", "qXF%d" % par, d_xnf[kb, :, :],
                        fmb[:, :, :].rearrange("p m q -> p (m q)"))
                else:
                    dma("sync", qn, fmb[:, :, :].rearrange("p m q -> p (m q)"),
                        d_xnf[kb, :, :])
                if qq == 0:
                    K.wait("tensor", qn)
                    for hc in range(HC):
                        mm(wbar_cols[:, hc:hc + 1], fmb[:, hc, :], PB[4][0:1, :],
                           start=(hc == 0), stop=(hc == HC - 1), inc=(hc == HC - 1))
                    K.wait("vector", "sT")
                    K.wait("vector", "qDBF")
                    vop(lambda e, s: e.tensor_copy(d_row_bf[0:1, :], PB[4][0:1, :]),
                        inc=True)
                    K.wait("gpsimd", "sV")
                    dma("gpsimd", "qDBF", d_dbf[0:1, kb * 512:(kb + 1) * 512],
                        d_row_bf[0:1, :])
                K.wait("tensor", qn)
                if qq == 0:
                    K.wait("tensor", "sV")
                elif kb >= 2:
                    for sem, val in snapsc[par]:
                        K.wait("tensor", sem, val)
                drow = d_row_bf[0:1, :] if qq == 0 else d_full[0:1, kb * 512:(kb + 1) * 512]
                for qt in range(2):
                    pb = PB[2 * par + qt]
                    for hc in range(HC):
                        mm(z_sb[:, hc, qt * 128:(qt + 1) * 128], fmb[:, hc, :],
                           pb[:, :], start=(hc == 0), stop=False)
                    mm(ones_bf, drow, pb[:, :], start=False, stop=True, inc=True)
                snap5[par] = ([("sT", K.count("sT"))] +
                              ([("qXF%d" % par, K.count("qXF%d" % par))] if qq == 0 else []))
                K.wait("vector", "sT")
                for qt in range(2):
                    pb = PB[2 * par + qt]
                    vop(lambda e, s, pb=pb, qt=qt, kb=kb: e.tensor_scalar(
                        scores[qt][:, kb * 512:(kb + 1) * 512], pb[:, :], 1.0, -1e30,
                        ALU.mult, ALU.max, accum_out=col(MTMP)), inc=True)
                    K.wait("vector", "sV")
                    vop(lambda e, s, qt=qt: e.tensor_tensor(
                        col(MCOL0 if qt == 0 else MCOL1), col(MCOL0 if qt == 0 else MCOL1),
                        col(MTMP), ALU.max), inc=True)
                snapsc = getattr(K, "_snapsc", {})
                snapsc[par] = [("sV", K.count("sV"))]
                K._snapsc = snapsc
            snapsc = K._snapsc
            vop(lambda e, s: e.tensor_scalar(col(NEGM0), col(MCOL0), -1.0, None, ALU.mult))
            vop(lambda e, s: e.tensor_scalar(col(NEGM1), col(MCOL1), -1.0, None, ALU.mult),
                inc=True)
            K.wait("scalar", "sV")
            K.wait("scalar", "qPT")
            for qt in range(2):
                aop(lambda e, s, qt=qt: e.activation(
                    p_bf[qt][:, :], scores[qt][:, :], AF.Exp,
                    bias=col(NEGM0 if qt == 0 else NEGM1),
                    accum_out=col(LCOL0 if qt == 0 else LCOL1)), inc=True)
            K.wait("vector", "sC")
            vop(lambda e, s: e.reciprocal(col(LINV0), col(LCOL0)))
            vop(lambda e, s: e.reciprocal(col(LINV1), col(LCOL1)), inc=True)
            K.wait("sync", "sC")
            K.wait("sync", "sT")
            for qt in range(2):
                dmat("qPT", pT[:, :, qt * 128:(qt + 1) * 128], p_bf[qt][:, :])
            for kc in range(32):
                par = kc % 4
                xb_t = xnb_tile[par]
                qn = "qXT%d" % par
                if par in snapx:
                    for sem, val in snapx[par]:
                        K.wait("scalar", sem, val)
                dma("scalar", qn, xb_t, d_xnb[kc * 128:(kc + 1) * 128, :])
                K.wait("tensor", qn)
                if kc == 0:
                    K.wait("tensor", "qPT")
                    K.wait("tensor", "sV")
                for qt in range(2):
                    for hbk in range(4):
                        mm(pT[:, kc, qt * 128:(qt + 1) * 128],
                           xb_t[:, hbk * 512:(hbk + 1) * 512],
                           PB[qt * 4 + hbk][:, :], start=(kc == 0), stop=(kc == 31),
                           inc=(qt == 1 and hbk == 3))
                snapx[par] = [("sT", K.count("sT"))]
            K.wait("vector", "sT")
            K.wait("vector", "qPX")
            for qt in range(2):
                for hbk in range(4):
                    vop(lambda e, s, qt=qt, hbk=hbk: e.tensor_scalar(
                        px_scr[:, hbk * 512:(hbk + 1) * 512], PB[qt * 4 + hbk][:, :],
                        col(LINV0 if qt == 0 else LINV1), None, ALU.mult),
                        inc=(hbk == 3))
                K.wait("sync", "sV")
                dmat("qPX", px_fm[:, :, qq * QQ + qt * 128: qq * QQ + (qt + 1) * 128],
                     px_scr[:, :])
                if qt == 0:
                    K.wait("vector", "qPX")
        K.barrier()
        dma("gpsimd", "qWV", wbig[:, :, :],
            wvt.rearrange("(c p) q -> p c q", p=128))
        K.wait("tensor", "qWV")
        bank = 0
        snapc = {}
        for hv in range(HC):
            slot = hv % 4
            if slot in snapc:
                for sem, val in snapc[slot]:
                    K.wait("tensor", sem, val)
            for hc in range(HC):
                for qb in range(2):
                    mm(wbig[:, hc, hv * 128:(hv + 1) * 128],
                       px_fm[:, hc, qb * 512:(qb + 1) * 512],
                       PB[(bank + qb) % 8][:, :],
                       start=(hc == 0), stop=(hc == HC - 1),
                       inc=(hc == HC - 1 and qb == 1))
            K.wait("vector", "sT")
            for qb in range(2):
                vop(lambda e, s, hv=hv, qb=qb, bank=bank: e.tensor_scalar(
                    ctx_fm[:, hv, qb * 512:(qb + 1) * 512],
                    PB[(bank + qb) % 8][:, :], bv_cols[:, hv:hv + 1], None, ALU.add),
                    inc=True)
            snapc[slot] = [("sV", K.count("sV"))]
            bank += 2
        K.barrier()
        dma("gpsimd", "qWO", wbig[:, :, :],
            wot.rearrange("(c p) q -> p c q", p=128))
        K.wait("tensor", "qWO")
        snapo = None
        snapf = {}
        for ob in range(4):
            K.wait("gpsimd", "sV")
            dma("gpsimd", "qBO", bo_bc,
                bov.rearrange("(o n) -> o n", o=1)[0:1, ob * 512:(ob + 1) * 512]
                .broadcast_to([128, 512]))

            def opj_load(t, ob=ob):
                par = t % 4
                if par in snapf:
                    for sem, val in snapf[par]:
                        K.wait("gpsimd", sem, val)
                dma("gpsimd", "qFI%d" % par, gf_tile[par],
                    d_gf[t, :, ob * 512:(ob + 1) * 512])
                dma("gpsimd", "qFI%d" % par, xo_blk[par],
                    xo[t * 128:(t + 1) * 128, ob * 512:(ob + 1) * 512])
            for hv in range(HC):
                if hv == 0 and snapo is not None:
                    K.wait("tensor", "sV", snapo)
                for t in range(TT):
                    mm(ctx_fm[:, hv, t * 128:(t + 1) * 128],
                       wbig[:, hv, ob * 512:(ob + 1) * 512], PB[t][:, :],
                       start=(hv == 0), stop=(hv == HC - 1),
                       inc=(hv == HC - 1 and t == TT - 1))
            K.wait("vector", "sT")
            K.wait("vector", "qBO")
            for t in range(TT):
                vop(lambda e, s, t=t: e.tensor_tensor(s1_big[:, t, :], PB[t][:, :],
                                                      bo_bc[:, :], ALU.add), inc=True)
            snapo = K.count("sV")
            for t in range(4):
                opj_load(t)
            for t in range(TT):
                par = t % 4
                K.wait("vector", "qFI%d" % par)
                K.wait("vector", "qOUT%d" % (t % 2))
                vop(lambda e, s, t=t, par=par: e.tensor_tensor(
                    s1_big[:, t, :], s1_big[:, t, :], gf_tile[par][:, :], ALU.mult))
                vop(lambda e, s, t=t, par=par: e.tensor_tensor(
                    out_scr[t % 2][:, :], s1_big[:, t, :], xo_blk[par][:, :], ALU.add),
                    inc=True)
                K.wait("gpsimd", "sV")
                dma("gpsimd", "qOUT%d" % (t % 2),
                    out[t * 128:(t + 1) * 128, ob * 512:(ob + 1) * 512],
                    out_scr[t % 2][:, :])
                snapf[par] = [("sV", K.count("sV"))]
                if t + 4 < TT:
                    opj_load(t + 4)
        K.wait("gpsimd", "qOUT0")
        K.wait("gpsimd", "qOUT1")
        K.wait("gpsimd", "qGF")
        K.wait("gpsimd", "qQG0")
        K.wait("gpsimd", "qQG1")
        K.wait("gpsimd", "qSC")

        sems = {n: stack.enter_context(nc.semaphore(n)) for n in sorted(K.sem_names)}
        block = stack.enter_context(nc.Block())

        @block.sync
        def _(e):
            K.replay("sync", e, sems)

        @block.scalar
        def _(e):
            K.replay("scalar", e, sems)

        @block.vector
        def _(e):
            K.replay("vector", e, sems)

        @block.tensor
        def _(e):
            K.replay("tensor", e, sems)

        @block.gpsimd
        def _(e):
            K.replay("gpsimd", e, sems)

    return nc


_NC_CACHE = [None]


def kernel(**inputs):
    x = np.asarray(inputs["L0_attn_hidden_states"], dtype=np.float32)
    wqkv = np.asarray(inputs["L0_attn_w_qkv"], dtype=np.float32)
    bqkv = np.asarray(inputs["L0_attn_b_qkv"], dtype=np.float32)
    w_out = np.asarray(inputs["L0_attn_w_out"], dtype=np.float32)
    b_out = np.asarray(inputs["L0_attn_b_out"], dtype=np.float32)
    w_gate = np.asarray(inputs["L0_ffn_w_gate"], dtype=np.float32)
    w_up = np.asarray(inputs["L0_ffn_w_up"], dtype=np.float32)
    w_down = np.asarray(inputs["L0_ffn_w_down"], dtype=np.float32)
    dperm = np.ascontiguousarray(
        np.asarray(inputs["L0_ffn_down_reorder_index"]).astype(np.int32))

    wq_a = np.ascontiguousarray(wqkv[0:H])
    wk_a = np.ascontiguousarray(wqkv[H:2 * H])
    wvt_a = np.ascontiguousarray(wqkv[2 * H:3 * H].T)
    wot_a = np.ascontiguousarray(w_out.T)
    bq_a = np.ascontiguousarray(bqkv[0:H])
    bv_a = np.ascontiguousarray(bqkv[2 * H:3 * H])
    bo_a = np.ascontiguousarray(b_out)
    wg_a = np.ascontiguousarray(w_gate)
    wu_a = np.ascontiguousarray(w_up)
    wd_a = np.ascontiguousarray(w_down)

    if _NC_CACHE[0] is None:
        _NC_CACHE[0] = build_program()
    nc = _NC_CACHE[0]

    in_maps = []
    for c in range(8):
        b, qb = c // 4, c % 4
        in_maps.append(dict(
            xb=np.ascontiguousarray(x[b]),
            xo=np.ascontiguousarray(x[b, qb * TO:(qb + 1) * TO]),
            wq=wq_a, wk=wk_a, wvt=wvt_a, wot=wot_a,
            bqv=bq_a, bvv=bv_a, bov=bo_a,
            wg=wg_a, wu=wu_a, wd=wd_a, pidx=dperm,
        ))
    res = run_bass_kernel_spmd(nc, in_maps, core_ids=list(range(8)))
    outp = np.empty((2, S, H), dtype=np.float32)
    for c in range(8):
        b, qb = c // 4, c % 4
        outp[b, qb * TO:(qb + 1) * TO] = np.asarray(res.results[c]["out"])
    return outp


# version 18
# speedup vs baseline: 1.0173x; 1.0173x over previous
import numpy as np
import ml_dtypes
import concourse.bass as bass
import concourse.mybir as mybir
from concourse.bass import IndirectOffsetOnAxis
from concourse.bass_utils import run_bass_kernel_spmd

F32 = mybir.dt.float32
BF16 = mybir.dt.bfloat16
FP8 = mybir.dt.float8e4
U32 = mybir.dt.uint32
I32 = mybir.dt.int32
AF = mybir.ActivationFunctionType
AX = mybir.AxisListType
ALU = mybir.AluOpType

H = 2048
HC = 16
S = 4096
STL = 32
TO = 1024
TT = 8
I = 5632
ICH = 44
IBL = 11
NMAC = 2
MTT = 4
QQ = 256
NQQ = 4
RSQH = 1.0 / float(np.sqrt(np.float32(2048.0)))
ENGS = ("sync", "scalar", "vector", "tensor", "gpsimd")
KBy = 1024


class KB:

    def __init__(self):
        self.ops = {e: [] for e in ENGS}
        self.cnt = {}
        self.sem_names = set()

    def count(self, sem):
        return self.cnt.get(sem, 0)

    def add(self, eng, fn, inc=None):
        if inc is not None:
            sem, n = inc
            self.sem_names.add(sem)
            self.cnt[sem] = self.cnt.get(sem, 0) + n

            def wrapped(e, sems, fn=fn, sem=sem, n=n):
                fn(e, sems).then_inc(sems[sem], n)
            self.ops[eng].append(wrapped)
        else:
            self.ops[eng].append(fn)

    def wait(self, eng, sem, val=None):
        self.sem_names.add(sem)
        if val is None:
            val = self.count(sem)
        if val <= 0:
            return

        def w(e, sems, sem=sem, val=val):
            e.wait_ge(sems[sem], val)
        self.ops[eng].append(w)

    def barrier(self):
        snap = dict(self.cnt)
        for eng in ENGS:
            for sem, val in snap.items():
                self.wait(eng, sem, val)

    def replay(self, eng_name, e, sems):
        for fn in self.ops[eng_name]:
            fn(e, sems)


def build_program():
    nc = bass.Bass("TRN2", target_bir_lowering=False)

    xb = nc.dram_tensor("xb", [S, H], F32, kind="ExternalInput").ap()
    xo = nc.dram_tensor("xo", [TO, H], F32, kind="ExternalInput").ap()
    wq = nc.dram_tensor("wq", [H, H], F32, kind="ExternalInput").ap()
    wk = nc.dram_tensor("wk", [H, H], F32, kind="ExternalInput").ap()
    wvt = nc.dram_tensor("wvt", [H, H], F32, kind="ExternalInput").ap()
    wot = nc.dram_tensor("wot", [H, H], F32, kind="ExternalInput").ap()
    bqv = nc.dram_tensor("bqv", [H], F32, kind="ExternalInput").ap()
    bvv = nc.dram_tensor("bvv", [H], F32, kind="ExternalInput").ap()
    bov = nc.dram_tensor("bov", [H], F32, kind="ExternalInput").ap()
    wg = nc.dram_tensor("wg", [I, H], F32, kind="ExternalInput").ap()
    wu = nc.dram_tensor("wu", [I, H], F32, kind="ExternalInput").ap()
    wd = nc.dram_tensor("wd", [H, I], F32, kind="ExternalInput").ap()
    pidx = nc.dram_tensor("pidx", [I], I32, kind="ExternalInput").ap()
    out = nc.dram_tensor("out", [TO, H], F32, kind="ExternalOutput").ap()

    d_xnb = nc.dram_tensor("d_xnb", [S, H], BF16).ap()
    d_wqb = nc.dram_tensor("d_wqb", [H, H], BF16).ap()
    d_qg = nc.dram_tensor("d_qg", [I, H], BF16).ap()
    d_qu = nc.dram_tensor("d_qu", [I, H], BF16).ap()
    d_wdq = nc.dram_tensor("d_wdq", [H, I], BF16).ap()
    d_wdt = nc.dram_tensor("d_wdt", [I, H], BF16).ap()
    d_qt = nc.dram_tensor("d_qt", [512, I], BF16).ap()
    d_z = nc.dram_tensor("d_z", [HC, 128, TO], BF16).ap()
    d_gf = nc.dram_tensor("d_gf", [TT, 128, H], BF16).ap()
    d_dbf = nc.dram_tensor("d_dbf", [1, S], BF16).ap()
    d_xnf = nc.dram_tensor("d_xnf", [8, 128, HC * 512], BF16).ap()
    d_wsg = nc.dram_tensor("d_wsg", [1, I], F32).ap()
    d_wsu = nc.dram_tensor("d_wsu", [1, I], F32).ap()
    d_wsd = nc.dram_tensor("d_wsd", [1, H], F32).ap()

    K = KB()
    from contextlib import ExitStack
    stack = ExitStack()
    with stack:
        AR = 200 * KBy
        arena = stack.enter_context(nc.sbuf_tensor("arena", [128, AR], mybir.dt.uint8))

        def view(off, dtype, shape):
            n = int(np.prod(shape[1:]))
            nb = n * mybir.dt.size(dtype)
            assert off % 64 == 0 and off + nb <= AR, (off, nb)
            v = arena[:, off:off + nb].bitcast(dtype)
            if len(shape) == 2:
                return v
            return v.rearrange("p (m q) -> p m q", q=shape[2])

        ones_bf = view(0, BF16, [128, 128])[0:1, :]
        pcols = view(256, U32, [128, ICH])
        bq_cols = view(448, BF16, [128, HC])
        bv_cols = view(512, F32, [128, HC])
        wbar_cols = view(576, BF16, [128, HC])
        sx_cols = view(640, F32, [128, TT])
        st_cols = view(704, F32, [128, TT])
        smalls = view(768, F32, [128, 64])
        (SUM, SSQ, MEAN, RSTD, VARC, RMX, RCP, C2, MTMP, MCOL0, MCOL1, NEGM0,
         NEGM1, LCOL0, LCOL1, LINV0, LINV1, TMPA, TMPB,
         RMX1, RCP1, C21, TMPA1, SUM1, SSQ1, MEAN1, RSTD1, VARC1, TMPB1,
         RMX2, RCP2, C22, TMPA2, RMX3, RCP3, C23, TMPA3,
         RMX4, RCP4, C24, TMPA4, RMX5, RCP5, C25, TMPA5) = range(45)

        def col(i):
            return smalls[:, i:i + 1]

        B = 2 * KBy

        wk_bf = view(B, BF16, [128, HC, H])
        A_sb = view(B + 64 * KBy, BF16, [128, HC, H])
        wq_cb = [view(B + 130 * KBy + i * 4 * KBy, BF16, [128, HC, 128]) for i in range(2)]

        xt_bf = [view(B + 138 * KBy + i * 4 * KBy, BF16, [128, H]) for i in range(2)]
        sq_scr = view(B + 146 * KBy, BF16, [128, H])
        disc_bf = view(B + 150 * KBy, BF16, [128, H])
        xt_f32 = [view(B + 154 * KBy + i * 8 * KBy, F32, [128, H]) for i in range(2)]
        wk_f32 = [view(B + 170 * KBy + i * 8 * KBy, F32, [128, H]) for i in range(2)]

        xo_f32 = [view(B + i * 8 * KBy, F32, [128, H]) for i in range(2)]
        xn_scr = view(B + 16 * KBy, F32, [128, H])
        sq3_scr = view(B + 24 * KBy, BF16, [128, H])
        qx8_tm = view(B + 28 * KBy, FP8, [128, H])
        qx_bf_tm = view(B + 30 * KBy, BF16, [128, H])
        tb_scr = view(B + 34 * KBy, BF16, [128, HC, 128])
        z_scr = [view(B + 38 * KBy + i * KBy, BF16, [128, 512]) for i in range(2)]
        disc3 = view(B + 40 * KBy, BF16, [128, H])
        xn_own_fm = view(B + 128 * KBy, BF16, [128, HC, TO])
        qx_fm8 = view(B + 160 * KBy, FP8, [128, HC, TO])

        wt_f32 = ([view(B + 44 * KBy + i * 8 * KBy, F32, [128, H]) for i in range(2)]
                  + [view(B + 176 * KBy + i * 8 * KBy, F32, [128, H]) for i in range(2)])
        q8_scr = ([view(B + 60 * KBy + i * 2 * KBy, FP8, [128, H]) for i in range(2)]
                  + [view(B + 192 * KBy + i * 2 * KBy, FP8, [128, H]) for i in range(2)])
        wdp_f32 = [view(B, F32, [128, I]), view(B + 176 * KBy, F32, [128, I])]
        q8_big = [view(B + 22 * KBy + i * 6 * KBy, FP8, [128, I]) for i in range(2)]
        wdt_scr = [view(B + 152 * KBy + i * 4 * KBy, BF16, [128, H]) for i in range(2)]

        t2v = view(B, F32, [128, MTT, I])
        g_rhsb = [view(B + 88 * KBy + i * 16 * KBy, BF16, [128, HC, 512]) for i in range(2)]
        u_rhsb = [view(B + 120 * KBy + i * 16 * KBy, BF16, [128, HC, 512]) for i in range(2)]
        qt_fm = view(B + 88 * KBy, BF16, [128, ICH, 512])
        qt8_tm = view(B + 152 * KBy, FP8, [128, I])
        wd_rhs = [view(B + 188 * KBy + i * 2 * KBy, BF16, [128, 1024]) for i in range(2)]
        wsg_bc = [view(B + 176 * KBy + i * 2 * KBy, F32, [128, 512]) for i in range(2)]
        wsu_bc = [view(B + 180 * KBy + i * 2 * KBy, F32, [128, 512]) for i in range(2)]
        wsd_bc = [view(B + 184 * KBy + i * 2 * KBy, F32, [128, 512]) for i in range(2)]
        g1_scr = view(B + 188 * KBy, F32, [128, 512])
        sg_scr = view(B + 190 * KBy, F32, [128, 512])
        u1_scr = view(B + 192 * KBy, F32, [128, 512])
        y1_scr = view(B + 194 * KBy, F32, [128, 512])
        gf_scr = view(B + 196 * KBy, BF16, [128, 512])

        scores = [view(B + i * 16 * KBy, F32, [128, S]) for i in range(2)]
        p_bf = [view(B + 32 * KBy + i * 8 * KBy, BF16, [128, S]) for i in range(2)]
        pT = view(B + 48 * KBy, BF16, [128, 32, QQ])
        fm_dest = [view(B + 64 * KBy + i * 16 * KBy, BF16, [128, HC, 512]) for i in range(2)]
        px_fm = view(B + 96 * KBy, BF16, [128, HC, TO])
        ctx_fm = view(B + 128 * KBy, BF16, [128, HC, TO])
        px_scr = view(B + 160 * KBy, BF16, [128, H])
        z_sb = view(B + 164 * KBy, BF16, [128, HC, QQ])
        xnb_tile = [view(B + 172 * KBy + i * 4 * KBy, BF16, [128, H]) for i in range(4)]
        d_row_bf = view(B + 188 * KBy, BF16, [1, 512])
        d_full = view(B + 189 * KBy, BF16, [1, S])
        wbig = view(B, BF16, [128, HC, H])
        bo_bc = view(B + 184 * KBy, F32, [128, 512])
        s1_big = view(B + 64 * KBy, F32, [128, TT, 512])
        gf_tile = [view(B + 80 * KBy + i * KBy, BF16, [128, 512]) for i in range(4)]
        xo_blk = [view(B + 84 * KBy + i * 2 * KBy, F32, [128, 512]) for i in range(4)]
        out_scr = [view(B + 92 * KBy + i * 2 * KBy, F32, [128, 512]) for i in range(2)]

        PB = [stack.enter_context(nc.psum_tensor("pb%d" % i, [128, 512], F32))
              for i in range(8)]
        def dma(eng, sem, out_ap, in_ap, **kw):
            K.add(eng, lambda e, s, o=out_ap, i=in_ap, kw=kw:
                  e.dma_start(out=o, in_=i, **kw), inc=(sem, 16))

        def dmat(sem, out_ap, in_ap, eng="sync"):
            K.add(eng, lambda e, s, o=out_ap, i=in_ap:
                  e.dma_start_transpose(out=o, in_=i), inc=(sem, 16))

        def mm(lhsT, rhs, pout, start, stop, inc=False):
            K.add("tensor",
                  lambda e, s, l=lhsT, r=rhs, p=pout, st=start, sp=stop:
                  e.matmul(p, l, r, start=st, stop=sp),
                  inc=("sT", 1) if inc else None)

        def vop(fn, inc=False):
            K.wait("vector", "sV")
            K.add("vector", fn, inc=("sV", 1))

        def aop(fn, inc=False):
            K.wait("scalar", "sC")
            K.add("scalar", fn, inc=("sC", 1))

        def gop(sem, fn):
            K.add("gpsimd", fn, inc=(sem, 16))

        def dram_col(dram_row_ap):
            return dram_row_ap.rearrange("o (p q) -> p (o q)", q=1)

        vop(lambda e, s: e.memset(ones_bf, 1.0))
        for c in range(HC):
            dma("gpsimd", "qCW", d_wqb[c * 128:(c + 1) * 128, :],
                wq[c * 128:(c + 1) * 128, :])
        for j in range(ICH):
            dma("gpsimd", "qPC", pcols[:, j:j + 1],
                pidx[j * 128:(j + 1) * 128].rearrange("(p o) -> p o", o=1).bitcast(U32))
        for c in range(HC):
            dma("gpsimd", "qPC", bq_cols[:, c:c + 1],
                bqv[c * 128:(c + 1) * 128].rearrange("(p o) -> p o", o=1))
            dma("gpsimd", "qPC", bv_cols[:, c:c + 1],
                bvv[c * 128:(c + 1) * 128].rearrange("(p o) -> p o", o=1))

        def gen_p1():
            snapw = {}
            for c in range(HC):
                par = c % 2
                if par in snapw:
                    K.wait("sync", "sV", snapw[par])
                dma("sync", "qWK%d" % par, wk_f32[par], wk[c * 128:(c + 1) * 128, :])
                K.wait("vector", "qWK%d" % par)
                vop(lambda e, s, c=c, par=par:
                    e.tensor_copy(wk_bf[:, c, :], wk_f32[par][:, :]), inc=True)
                snapw[par] = K.count("sV")
            wk_done = K.count("sV")
            K.wait("sync", "qCW")
            bank = 0
            for hi in range(HC):
                cb = wq_cb[hi % 2]
                qn = "qWQ%d" % (hi % 2)
                if hi >= 2:
                    K.wait("sync", "sT")
                for o in range(HC):
                    dma("sync", qn, cb[:, o, :],
                        d_wqb[o * 128:(o + 1) * 128, hi * 128:(hi + 1) * 128])
                K.wait("tensor", qn)
                K.wait("tensor", "sV", wk_done)
                for hob in range(4):
                    pb = PB[bank % 8]
                    if bank >= 8:
                        K.wait("tensor", "sV")
                    for o in range(HC):
                        mm(cb[:, o, :], wk_bf[:, o, hob * 512:(hob + 1) * 512],
                           pb[:, :], start=(o == 0), stop=(o == HC - 1),
                           inc=(o == HC - 1))
                    K.wait("vector", "sT")
                    vop(lambda e, s, pb=pb, hi=hi, hob=hob:
                        e.tensor_copy(A_sb[:, hi, hob * 512:(hob + 1) * 512], pb[:, :]),
                        inc=True)
                    bank += 1
            K.wait("tensor", "qPC")
            for hob in range(4):
                pb = PB[bank % 8]
                K.wait("tensor", "sV")
                for o in range(HC):
                    mm(bq_cols[:, o:o + 1], wk_bf[:, o, hob * 512:(hob + 1) * 512],
                       pb[0:1, :], start=(o == 0), stop=(o == HC - 1), inc=(o == HC - 1))
                K.wait("vector", "sT")
                vop(lambda e, s, pb=pb: e.tensor_scalar(
                    d_row_bf[0:1, :], pb[0:1, :], RSQH, None, ALU.mult), inc=True)
                K.wait("gpsimd", "sV")
                dma("gpsimd", "qDBF", d_dbf[0:1, hob * 512:(hob + 1) * 512], d_row_bf[0:1, :])
                K.wait("vector", "qDBF")
                bank += 1
                yield
            K.wait("gpsimd", "qDBF")
            for c in range(HC):
                dma("gpsimd", "qPC", wbar_cols[:, c:c + 1],
                    dram_col(d_dbf[0:1, c * 128:(c + 1) * 128]))

        def gen_p2():
            snap2 = {}
            for t in range(STL):
                par = t % 2
                buf = xt_f32[par]
                obuf = xt_bf[par]
                qn = "qX2%d" % par
                cS, cQ, cM, cV, cR, cT = ((SUM, SSQ, MEAN, VARC, RSTD, TMPB) if par == 0
                                          else (SUM1, SSQ1, MEAN1, VARC1, RSTD1, TMPB1))
                cA = TMPA if par == 0 else TMPA1
                if t >= 2:
                    for sem, val in snap2[par]:
                        K.wait("gpsimd", sem, val)
                dma("gpsimd", qn, buf, xb[t * 128:(t + 1) * 128, :])
                K.wait("scalar", qn)
                aop(lambda e, s, b=buf, cQ=cQ: e.activation(sq_scr[:, :], b, AF.Square,
                                                            accum_out=col(cQ)))
                K.wait("vector", qn)
                vop(lambda e, s, b=buf, cS=cS, par=par: e.tensor_scalar(
                    disc_bf[:, :], b, 1.0, 0.0, ALU.mult, ALU.add, accum_out=col(cS)))
                K.wait("vector", "sC")
                vop(lambda e, s, cM=cM, cS=cS: e.tensor_scalar(col(cM), col(cS), 1.0 / H, None, ALU.mult))
                vop(lambda e, s, cV=cV, cQ=cQ: e.tensor_scalar(col(cV), col(cQ), 1.0 / H, None, ALU.mult))
                vop(lambda e, s, cA=cA, cM=cM: e.tensor_tensor(col(cA), col(cM), col(cM), ALU.mult))
                vop(lambda e, s, cV=cV, cA=cA: e.tensor_tensor(col(cV), col(cV), col(cA), ALU.subtract))
                vop(lambda e, s, cV=cV: e.tensor_scalar(col(cV), col(cV), 1e-5, None, ALU.add),
                    inc=True)
                K.wait("scalar", "sV")
                aop(lambda e, s, cT=cT, cV=cV: e.activation(col(cT), col(cV), AF.Sqrt))
                K.wait("vector", "sC")
                vop(lambda e, s, cR=cR, cT=cT: e.reciprocal(col(cR), col(cT)))
                vop(lambda e, s, b=buf, ob=obuf, cM=cM, cR=cR: e.tensor_scalar(
                    ob, b, col(cM), col(cR), ALU.subtract, ALU.mult), inc=True)
                K.wait("scalar", "sV")
                dma("scalar", "qNB%d" % par, d_xnb[t * 128:(t + 1) * 128, :], obuf)
                snap2[par] = [("sV", K.count("sV")), ("sC", K.count("sC")),
                              ("qNB%d" % par, K.count("qNB%d" % par))]
                yield

        def gen_p3():
            snap3 = {}
            for t in range(TT):
                par = t % 2
                buf = xo_f32[par]
                qn = "qX3%d" % par
                cS, cQ, cM, cV, cR, cT = ((SUM, SSQ, MEAN, VARC, RSTD, TMPB) if par == 0
                                          else (SUM1, SSQ1, MEAN1, VARC1, RSTD1, TMPB1))
                cA = TMPA if par == 0 else TMPA1
                rmx, rcp, c2 = (RMX, RCP, C2) if par == 0 else (RMX1, RCP1, C21)
                if t >= 2:
                    for sem, val in snap3[par]:
                        K.wait("gpsimd", sem, val)
                dma("gpsimd", qn, buf, xo[t * 128:(t + 1) * 128, :])
                K.wait("scalar", qn)
                aop(lambda e, s, b=buf, cQ=cQ: e.activation(sq3_scr[:, :], b, AF.Square,
                                                            accum_out=col(cQ)))
                K.wait("vector", qn)
                vop(lambda e, s, b=buf, cS=cS: e.tensor_scalar(
                    disc3[:, :], b, 1.0, 0.0, ALU.mult, ALU.add, accum_out=col(cS)))
                K.wait("vector", "sC")
                vop(lambda e, s, cM=cM, cS=cS: e.tensor_scalar(col(cM), col(cS), 1.0 / H, None, ALU.mult))
                vop(lambda e, s, cV=cV, cQ=cQ: e.tensor_scalar(col(cV), col(cQ), 1.0 / H, None, ALU.mult))
                vop(lambda e, s, cA=cA, cM=cM: e.tensor_tensor(col(cA), col(cM), col(cM), ALU.mult))
                vop(lambda e, s, cV=cV, cA=cA: e.tensor_tensor(col(cV), col(cV), col(cA), ALU.subtract))
                vop(lambda e, s, cV=cV: e.tensor_scalar(col(cV), col(cV), 1e-5, None, ALU.add),
                    inc=True)
                K.wait("scalar", "sV")
                aop(lambda e, s, cT=cT, cV=cV: e.activation(col(cT), col(cV), AF.Sqrt))
                K.wait("vector", "sC")
                vop(lambda e, s, cR=cR, cT=cT: e.reciprocal(col(cR), col(cT)))
                K.wait("vector", "qFM")
                K.wait("vector", "qTB")
                K.wait("vector", "sC")
                vop(lambda e, s, b=buf, cM=cM, cR=cR: e.tensor_scalar(
                    xn_scr[:, :], b, col(cM), col(cR), ALU.subtract, ALU.mult))
                vop(lambda e, s: e.tensor_copy(qx_bf_tm[:, :], xn_scr[:, :]))
                vop(lambda e, s, rmx=rmx: e.reduce_max(col(rmx), xn_scr[:, :], AX.X,
                                                       apply_absolute_value=True))
                vop(lambda e, s, rmx=rmx, rcp=rcp: e.reciprocal(col(rcp), col(rmx)))
                vop(lambda e, s, rcp=rcp, c2=c2: e.tensor_scalar(col(c2), col(rcp), 224.0,
                                                                 None, ALU.mult))
                vop(lambda e, s, t=t, rmx=rmx: e.tensor_scalar(
                    sx_cols[:, t:t + 1], col(rmx), 1.0 / 224.0, None, ALU.mult), inc=True)
                K.wait("sync", "sV")
                dmat("qFM", xn_own_fm[:, :, t * 128:(t + 1) * 128], qx_bf_tm[:, :])
                K.wait("scalar", "sV")
                aop(lambda e, s, c2=c2: e.activation(qx8_tm[:, :], xn_scr[:, :], AF.Copy,
                                                     scale=col(c2)))
                K.wait("vector", "sC")
                K.wait("vector", "qFM")
                vop(lambda e, s: e.tensor_copy(qx_bf_tm[:, :], qx8_tm[:, :]), inc=True)
                K.wait("sync", "sV")
                K.wait("sync", "sC")
                dmat("qTB", tb_scr[:, :, :], qx_bf_tm[:, :])
                K.wait("scalar", "qTB")
                aop(lambda e, s, t=t: e.activation(
                    qx_fm8[:, :, t * 128:(t + 1) * 128], tb_scr[:, :, :], AF.Copy))
                snap3[par] = [("sV", K.count("sV")), ("sC", K.count("sC"))]
                yield

        def gen_p35():
            K.wait("tensor", "qFM")
            snapz = {}
            bank = 0
            for ho in range(HC):
                for qb in range(2):
                    pb = PB[bank % 8]
                    par = bank % 2
                    if bank >= 8:
                        K.wait("tensor", "sC")
                    for hi in range(HC):
                        mm(A_sb[:, hi, ho * 128:(ho + 1) * 128],
                           xn_own_fm[:, hi, qb * 512:(qb + 1) * 512],
                           pb[:, :], start=(hi == 0), stop=(hi == HC - 1),
                           inc=(hi == HC - 1))
                    K.wait("scalar", "sT")
                    if par in snapz:
                        for sem, val in snapz[par]:
                            K.wait("scalar", sem, val)
                    aop(lambda e, s, pb=pb, par=par: e.activation(
                        z_scr[par][:, :], pb[:, :], AF.Copy, scale=RSQH))
                    K.wait("gpsimd", "sC")
                    dma("gpsimd", "qZS%d" % par, d_z[ho, :, qb * 512:(qb + 1) * 512],
                        z_scr[par][:, :])
                    snapz[par] = [("qZS%d" % par, K.count("qZS%d" % par))]
                    bank += 1
                    yield

        def gen_p4a():
            cols = [(RMX2, RCP2, C22, TMPA2), (RMX3, RCP3, C23, TMPA3),
                    (RMX4, RCP4, C24, TMPA4), (RMX5, RCP5, C25, TMPA5)]
            snapa = {}
            for wi, (wsrc, qdst, wsdst) in enumerate(((wg, d_qg, d_wsg), (wu, d_qu, d_wsu))):
                for j in range(ICH):
                    jj = wi * ICH + j
                    par = jj % 4
                    buf = wt_f32[par]
                    qn = "qW4%d" % par
                    rmx, rcp, c2, tmpa = cols[par]
                    if par in snapa:
                        for sem, val in snapa[par]:
                            K.wait("sync", sem, val)
                    dma("sync", qn, buf, wsrc[j * 128:(j + 1) * 128, :])
                    K.wait("vector", qn)
                    vop(lambda e, s, b=buf, rmx=rmx: e.reduce_max(
                        col(rmx), b, AX.X, apply_absolute_value=True))
                    vop(lambda e, s, rmx=rmx, rcp=rcp: e.reciprocal(col(rcp), col(rmx)))
                    vop(lambda e, s, rcp=rcp, c2=c2: e.tensor_scalar(
                        col(c2), col(rcp), 224.0, None, ALU.mult))
                    vop(lambda e, s, rmx=rmx, tmpa=tmpa: e.tensor_scalar(
                        col(tmpa), col(rmx), 1.0 / 224.0, None, ALU.mult), inc=True)
                    K.wait("gpsimd", "sV")
                    dma("gpsimd", "qWSA%d" % par, dram_col(wsdst[0:1, j * 128:(j + 1) * 128]),
                        col(tmpa))
                    K.wait("scalar", "sV")
                    K.wait("scalar", "qQG%d" % par)
                    aop(lambda e, s, b=buf, c2=c2, par=par: e.activation(
                        q8_scr[par][:, :], b, AF.Copy, scale=col(c2)))
                    K.wait("gpsimd", "sC")
                    dma("gpsimd", "qQG%d" % par, qdst[j * 128:(j + 1) * 128, :],
                        q8_scr[par][:, :])
                    snapa[par] = [("sV", K.count("sV")), ("sC", K.count("sC")),
                                  ("qQG%d" % par, K.count("qQG%d" % par)),
                                  ("qWSA%d" % par, K.count("qWSA%d" % par))]
                    yield
        def gen_p4b():
            for sem in ("sV", "sC", "qTB", "qFM", "qQG2", "qQG3"):
                K.wait("sync", sem)
                K.wait("scalar", sem)
            snapb = {}
            cols = [(RMX, RCP, C2, TMPA), (RMX1, RCP1, C21, TMPA1)]
            for j in range(HC):
                par = j % 2
                buf = wdp_f32[par]
                qn = "qW5%d" % par
                rmx, rcp, c2, tmpa = cols[par]
                if par in snapb:
                    for sem, val in snapb[par]:
                        K.wait("sync", sem, val)
                dma("sync", qn, buf, wd[j * 128:(j + 1) * 128, :])
                K.wait("vector", qn)
                vop(lambda e, s, b=buf, rmx=rmx: e.reduce_max(
                    col(rmx), b, AX.X, apply_absolute_value=True))
                vop(lambda e, s, rmx=rmx, rcp=rcp: e.reciprocal(col(rcp), col(rmx)))
                vop(lambda e, s, rcp=rcp, c2=c2: e.tensor_scalar(
                    col(c2), col(rcp), 224.0, None, ALU.mult))
                vop(lambda e, s, rmx=rmx, tmpa=tmpa: e.tensor_scalar(
                    col(tmpa), col(rmx), 1.0 / 224.0, None, ALU.mult), inc=True)
                K.wait("gpsimd", "sV")
                dma("gpsimd", "qWSA%d" % par, dram_col(d_wsd[0:1, j * 128:(j + 1) * 128]),
                    col(tmpa))
                K.wait("scalar", "sV")
                K.wait("scalar", "qQG%d" % par)
                aop(lambda e, s, b=buf, c2=c2, par=par: e.activation(
                    q8_big[par][:, :], b, AF.Copy, scale=col(c2)))
                K.wait("gpsimd", "sC")
                dma("gpsimd", "qQG%d" % par, d_wdq[j * 128:(j + 1) * 128, :],
                    q8_big[par][:, :])
                snapb[par] = [("sV", K.count("sV")), ("sC", K.count("sC")),
                              ("qQG%d" % par, K.count("qQG%d" % par)),
                              ("qWSA%d" % par, K.count("qWSA%d" % par))]
                yield
        def gen_p4c():
            for j in range(ICH):
                buf = wdt_scr[j % 2]
                qn = "qT4%d" % (j % 2)
                if j >= 2:
                    K.wait("sync", "qSC")
                dmat(qn, buf, d_wdq[:, j * 128:(j + 1) * 128])
                K.wait("gpsimd", qn)
                gop("qSC", lambda e, s, b=buf, j=j: e.indirect_dma_start(
                    out=d_wdt, out_offset=IndirectOffsetOnAxis(ap=pcols[:, j:j + 1], axis=0),
                    in_=b, in_offset=None))
                yield

        def gen_p4d():
            snapg = {}
            for mac in range(NMAC):
                tts = [mac * MTT + i for i in range(MTT)]
                for ib in range(IBL):
                    gib = mac * IBL + ib
                    par = gib % 2
                    K.wait("gpsimd", "sV")
                    dma("gpsimd", "qWS", wsg_bc[ib % 2],
                        d_wsg[0:1, ib * 512:(ib + 1) * 512].broadcast_to([128, 512]))
                    dma("gpsimd", "qWS", wsu_bc[ib % 2],
                        d_wsu[0:1, ib * 512:(ib + 1) * 512].broadcast_to([128, 512]))
                    qn = "qGR%d" % par
                    if ib == 0 and mac > 0:
                        K.wait("sync", "sT")
                    if par in snapg:
                        for sem, val in snapg[par]:
                            K.wait("sync", sem, val)
                    dmat(qn, g_rhsb[par][:, :, :], d_qg[ib * 512:(ib + 1) * 512, :])
                    dmat(qn, u_rhsb[par][:, :, :], d_qu[ib * 512:(ib + 1) * 512, :])
                    K.wait("tensor", qn)
                    for half in range(2):
                        hts = tts[half * 2:(half + 1) * 2]
                        hkey = "gh%d" % half
                        pbase = half * 4
                        if hkey in snapg:
                            K.wait("tensor", "sV", snapg[hkey])
                        for hc in range(HC):
                            for i_h, t in enumerate(hts):
                                mm(qx_fm8[:, hc, t * 128:(t + 1) * 128], g_rhsb[par][:, hc, :],
                                   PB[pbase + i_h][:, :], start=(hc == 0), stop=(hc == HC - 1))
                                mm(qx_fm8[:, hc, t * 128:(t + 1) * 128], u_rhsb[par][:, hc, :],
                                   PB[pbase + 2 + i_h][:, :], start=(hc == 0),
                                   stop=(hc == HC - 1),
                                   inc=(hc == HC - 1 and i_h == 1))
                        snapg[par] = [("sT", K.count("sT"))]
                        K.wait("vector", "sT")
                        K.wait("vector", "qWS")
                        for i_h, t in enumerate(hts):
                            vop(lambda e, s, pbase=pbase, i_h=i_h, t=t, ib=ib:
                                e.scalar_tensor_tensor(
                                    g1_scr[:, :], PB[pbase + i_h][:, :], sx_cols[:, t:t + 1],
                                    wsg_bc[ib % 2], ALU.mult, ALU.mult), inc=True)
                            K.wait("scalar", "sV")
                            aop(lambda e, s: e.activation(sg_scr[:, :], g1_scr[:, :], AF.Silu))
                            vop(lambda e, s, pbase=pbase, i_h=i_h, t=t, ib=ib:
                                e.scalar_tensor_tensor(
                                    u1_scr[:, :], PB[pbase + 2 + i_h][:, :],
                                    sx_cols[:, t:t + 1], wsu_bc[ib % 2],
                                    ALU.mult, ALU.mult))
                            K.wait("vector", "sC")
                            it_g = half * 2 + i_h
                            vop(lambda e, s, it_g=it_g, ib=ib: e.tensor_tensor(
                                t2v[:, it_g, ib * 512:(ib + 1) * 512], sg_scr[:, :],
                                u1_scr[:, :], ALU.mult), inc=True)
                        snapg[hkey] = K.count("sV")
                        yield
                for qi in range(4):
                    K.wait("gpsimd", "qQTF%d" % qi)
                for i_t, t in enumerate(tts):
                    vop(lambda e, s, i_t=i_t: e.reduce_max(
                        col(RMX), t2v[:, i_t, :], AX.X, apply_absolute_value=True), inc=True)
                    K.wait("vector", "sC")
                    vop(lambda e, s: e.reciprocal(col(RCP), col(RMX)))
                    vop(lambda e, s: e.tensor_scalar(col(C2), col(RCP), 224.0, None, ALU.mult))
                    vop(lambda e, s, t=t: e.tensor_scalar(
                        st_cols[:, t:t + 1], col(RMX), 1.0 / 224.0, None, ALU.mult), inc=True)
                    K.wait("scalar", "sV")
                    K.wait("scalar", "qQT")
                    K.wait("scalar", "qSC")
                    aop(lambda e, s, i_t=i_t: e.activation(qt8_tm[:, :], t2v[:, i_t, :],
                                                           AF.Copy, scale=col(C2)), inc=True)
                    K.wait("gpsimd", "sC")
                    dma("gpsimd", "qQT", d_qt[i_t * 128:(i_t + 1) * 128, :], qt8_tm[:, :])
                    yield
                K.wait("sync", "qQT")
                K.wait("sync", "sT")
                for ic in range(ICH):
                    dmat("qQTF%d" % (ic % 4), qt_fm[:, ic, :],
                         d_qt[:, ic * 128:(ic + 1) * 128])
                K.wait("sync", "qSC")
                K.wait("scalar", "qSC")
                K.wait("sync", "sV")
                qtf_snap = [K.count("qQTF%d" % (ic % 4)) for ic in range(ICH)]
                for hp in range(2):
                    hbs = (2 * hp, 2 * hp + 1)
                    K.wait("gpsimd", "sV")
                    for bi, hb in enumerate(hbs):
                        dma("gpsimd", "qWS", wsd_bc[bi],
                            d_wsd[0:1, hb * 512:(hb + 1) * 512].broadcast_to([128, 512]))
                    if hp == 1 or mac > 0:
                        K.wait("tensor", "sV")
                    for ic in range(ICH):
                        buf = wd_rhs[ic % 2]
                        qn = "qWD%d" % (ic % 2)
                        if ic >= 2 or hp == 1 or mac > 0:
                            K.wait("sync", "sT")
                        dma("sync", qn, buf,
                            d_wdt[ic * 128:(ic + 1) * 128,
                                  2 * hp * 512:(2 * hp + 2) * 512])
                        K.wait("tensor", qn)
                        K.wait("tensor", "qQTF%d" % (ic % 4), qtf_snap[ic])
                        for bi, hb in enumerate(hbs):
                            for i_t, t in enumerate(tts):
                                mm(qt_fm[:, ic, i_t * 128:(i_t + 1) * 128],
                                   buf[:, bi * 512:(bi + 1) * 512],
                                   PB[bi * 4 + i_t][:, :],
                                   start=(ic == 0), stop=(ic == ICH - 1),
                                   inc=(bi == 1 and i_t == MTT - 1))
                    K.wait("vector", "sT")
                    K.wait("vector", "qWS")
                    for bi, hb in enumerate(hbs):
                        for i_t, t in enumerate(tts):
                            vop(lambda e, s, bi=bi, i_t=i_t, t=t: e.scalar_tensor_tensor(
                                y1_scr[:, :], PB[bi * 4 + i_t][:, :], st_cols[:, t:t + 1],
                                wsd_bc[bi], ALU.mult, ALU.mult), inc=True)
                            K.wait("scalar", "sV")
                            K.wait("scalar", "qGF")
                            aop(lambda e, s: e.activation(gf_scr[:, :], y1_scr[:, :],
                                                          AF.Silu))
                            K.wait("gpsimd", "sC")
                            dma("gpsimd", "qGF", d_gf[t, :, hb * 512:(hb + 1) * 512],
                                gf_scr[:, :])
                            K.wait("vector", "sC")
                    yield
                    snapg["gh0"] = K.count("sV")
                    snapg["gh1"] = K.count("sV")
                    yield


        def ilv(*gens):
            gs = [(g, 1) if not isinstance(g, tuple) else g for g in gens]
            gs = [[g, w] for g, w in gs]
            while gs:
                for item in list(gs):
                    g, w = item
                    for _ in range(w):
                        try:
                            next(g)
                        except StopIteration:
                            gs.remove(item)
                            break

        ilv(gen_p1(), gen_p2())
        K.barrier()
        def chain(*gens):
            for g in gens:
                for _ in g:
                    yield
        ilv(chain(gen_p3(), gen_p35()), chain(gen_p4a(), gen_p4b()))
        K.barrier()
        ilv(gen_p4d(), (gen_p4c(), 4))
        K.barrier()
        snap5 = {}
        snapx = {}
        for qq in range(NQQ):
            K.wait("scalar", "sT")
            K.wait("scalar", "sC")
            for ho in range(HC):
                dma("scalar", "qZL", z_sb[:, ho, :], d_z[ho, :, qq * QQ:(qq + 1) * QQ])
            if qq > 0:
                K.wait("scalar", "qDBF")
                dma("scalar", "qDR", d_full[0:1, :], d_dbf[0:1, :])
            vop(lambda e, s: e.memset(col(MCOL0), -1e30))
            vop(lambda e, s: e.memset(col(MCOL1), -1e30))
            K.wait("vector", "sC")
            K.wait("tensor", "qZL")
            K.wait("tensor", "qDR")
            K.wait("vector", "qZL")
            for kb in range(8):
                par = kb % 2
                fmb = fm_dest[par]
                qn = "qFM%d" % par
                if par in snap5:
                    for sem, val in snap5[par]:
                        K.wait("sync", sem, val)
                if qq == 0:
                    dmat(qn, fmb, d_xnb[kb * 512:(kb + 1) * 512, :])
                    K.wait("gpsimd", qn)
                    dma("gpsimd", "qXF%d" % par, d_xnf[kb, :, :],
                        fmb[:, :, :].rearrange("p m q -> p (m q)"))
                else:
                    dma("sync", qn, fmb[:, :, :].rearrange("p m q -> p (m q)"),
                        d_xnf[kb, :, :])
                if qq == 0:
                    K.wait("tensor", qn)
                    for hc in range(HC):
                        mm(wbar_cols[:, hc:hc + 1], fmb[:, hc, :], PB[4][0:1, :],
                           start=(hc == 0), stop=(hc == HC - 1), inc=(hc == HC - 1))
                    K.wait("vector", "sT")
                    K.wait("vector", "qDBF")
                    vop(lambda e, s: e.tensor_copy(d_row_bf[0:1, :], PB[4][0:1, :]),
                        inc=True)
                    K.wait("gpsimd", "sV")
                    dma("gpsimd", "qDBF", d_dbf[0:1, kb * 512:(kb + 1) * 512],
                        d_row_bf[0:1, :])
                K.wait("tensor", qn)
                if qq == 0:
                    K.wait("tensor", "sV")
                elif kb >= 2:
                    for sem, val in snapsc[par]:
                        K.wait("tensor", sem, val)
                drow = d_row_bf[0:1, :] if qq == 0 else d_full[0:1, kb * 512:(kb + 1) * 512]
                for qt in range(2):
                    pb = PB[2 * par + qt]
                    for hc in range(HC):
                        mm(z_sb[:, hc, qt * 128:(qt + 1) * 128], fmb[:, hc, :],
                           pb[:, :], start=(hc == 0), stop=False)
                    mm(ones_bf, drow, pb[:, :], start=False, stop=True, inc=True)
                snap5[par] = ([("sT", K.count("sT"))] +
                              ([("qXF%d" % par, K.count("qXF%d" % par))] if qq == 0 else []))
                K.wait("vector", "sT")
                for qt in range(2):
                    pb = PB[2 * par + qt]
                    vop(lambda e, s, pb=pb, qt=qt, kb=kb: e.tensor_scalar(
                        scores[qt][:, kb * 512:(kb + 1) * 512], pb[:, :], 1.0, -1e30,
                        ALU.mult, ALU.max, accum_out=col(MTMP)), inc=True)
                    K.wait("vector", "sV")
                    vop(lambda e, s, qt=qt: e.tensor_tensor(
                        col(MCOL0 if qt == 0 else MCOL1), col(MCOL0 if qt == 0 else MCOL1),
                        col(MTMP), ALU.max), inc=True)
                snapsc = getattr(K, "_snapsc", {})
                snapsc[par] = [("sV", K.count("sV"))]
                K._snapsc = snapsc
            snapsc = K._snapsc
            vop(lambda e, s: e.tensor_scalar(col(NEGM0), col(MCOL0), -1.0, None, ALU.mult))
            vop(lambda e, s: e.tensor_scalar(col(NEGM1), col(MCOL1), -1.0, None, ALU.mult),
                inc=True)
            K.wait("scalar", "sV")
            K.wait("scalar", "qPT")
            for qt in range(2):
                aop(lambda e, s, qt=qt: e.activation(
                    p_bf[qt][:, :], scores[qt][:, :], AF.Exp,
                    bias=col(NEGM0 if qt == 0 else NEGM1),
                    accum_out=col(LCOL0 if qt == 0 else LCOL1)), inc=True)
            K.wait("vector", "sC")
            vop(lambda e, s: e.reciprocal(col(LINV0), col(LCOL0)))
            vop(lambda e, s: e.reciprocal(col(LINV1), col(LCOL1)), inc=True)
            K.wait("sync", "sC")
            K.wait("sync", "sT")
            for qt in range(2):
                dmat("qPT", pT[:, :, qt * 128:(qt + 1) * 128], p_bf[qt][:, :])
            for kc in range(32):
                par = kc % 4
                xb_t = xnb_tile[par]
                qn = "qXT%d" % par
                if par in snapx:
                    for sem, val in snapx[par]:
                        K.wait("scalar", sem, val)
                dma("scalar", qn, xb_t, d_xnb[kc * 128:(kc + 1) * 128, :])
                K.wait("tensor", qn)
                if kc == 0:
                    K.wait("tensor", "qPT")
                    K.wait("tensor", "sV")
                for qt in range(2):
                    for hbk in range(4):
                        mm(pT[:, kc, qt * 128:(qt + 1) * 128],
                           xb_t[:, hbk * 512:(hbk + 1) * 512],
                           PB[qt * 4 + hbk][:, :], start=(kc == 0), stop=(kc == 31),
                           inc=(qt == 1 and hbk == 3))
                snapx[par] = [("sT", K.count("sT"))]
            K.wait("vector", "sT")
            K.wait("vector", "qPX")
            for qt in range(2):
                for hbk in range(4):
                    vop(lambda e, s, qt=qt, hbk=hbk: e.tensor_scalar(
                        px_scr[:, hbk * 512:(hbk + 1) * 512], PB[qt * 4 + hbk][:, :],
                        col(LINV0 if qt == 0 else LINV1), None, ALU.mult),
                        inc=(hbk == 3))
                K.wait("sync", "sV")
                dmat("qPX", px_fm[:, :, qq * QQ + qt * 128: qq * QQ + (qt + 1) * 128],
                     px_scr[:, :])
                if qt == 0:
                    K.wait("vector", "qPX")
        K.barrier()
        dma("gpsimd", "qWV", wbig[:, :, :],
            wvt.rearrange("(c p) q -> p c q", p=128))
        K.wait("tensor", "qWV")
        bank = 0
        snapc = {}
        for hv in range(HC):
            slot = hv % 4
            if slot in snapc:
                for sem, val in snapc[slot]:
                    K.wait("tensor", sem, val)
            for hc in range(HC):
                for qb in range(2):
                    mm(wbig[:, hc, hv * 128:(hv + 1) * 128],
                       px_fm[:, hc, qb * 512:(qb + 1) * 512],
                       PB[(bank + qb) % 8][:, :],
                       start=(hc == 0), stop=(hc == HC - 1),
                       inc=(hc == HC - 1 and qb == 1))
            K.wait("vector", "sT")
            for qb in range(2):
                vop(lambda e, s, hv=hv, qb=qb, bank=bank: e.tensor_scalar(
                    ctx_fm[:, hv, qb * 512:(qb + 1) * 512],
                    PB[(bank + qb) % 8][:, :], bv_cols[:, hv:hv + 1], None, ALU.add),
                    inc=True)
            snapc[slot] = [("sV", K.count("sV"))]
            bank += 2
        K.barrier()
        dma("gpsimd", "qWO", wbig[:, :, :],
            wot.rearrange("(c p) q -> p c q", p=128))
        K.wait("tensor", "qWO")
        snapo = None
        snapf = {}
        for ob in range(4):
            K.wait("gpsimd", "sV")
            dma("gpsimd", "qBO", bo_bc,
                bov.rearrange("(o n) -> o n", o=1)[0:1, ob * 512:(ob + 1) * 512]
                .broadcast_to([128, 512]))

            def opj_load(t, ob=ob):
                par = t % 4
                if par in snapf:
                    for sem, val in snapf[par]:
                        K.wait("gpsimd", sem, val)
                dma("gpsimd", "qFI%d" % par, gf_tile[par],
                    d_gf[t, :, ob * 512:(ob + 1) * 512])
                dma("gpsimd", "qFI%d" % par, xo_blk[par],
                    xo[t * 128:(t + 1) * 128, ob * 512:(ob + 1) * 512])
            for hv in range(HC):
                if hv == 0 and snapo is not None:
                    K.wait("tensor", "sV", snapo)
                for t in range(TT):
                    mm(ctx_fm[:, hv, t * 128:(t + 1) * 128],
                       wbig[:, hv, ob * 512:(ob + 1) * 512], PB[t][:, :],
                       start=(hv == 0), stop=(hv == HC - 1),
                       inc=(hv == HC - 1 and t == TT - 1))
            K.wait("vector", "sT")
            K.wait("vector", "qBO")
            for t in range(TT):
                vop(lambda e, s, t=t: e.tensor_tensor(s1_big[:, t, :], PB[t][:, :],
                                                      bo_bc[:, :], ALU.add), inc=True)
            snapo = K.count("sV")
            for t in range(4):
                opj_load(t)
            for t in range(TT):
                par = t % 4
                K.wait("vector", "qFI%d" % par)
                K.wait("vector", "qOUT%d" % (t % 2))
                vop(lambda e, s, t=t, par=par: e.tensor_tensor(
                    s1_big[:, t, :], s1_big[:, t, :], gf_tile[par][:, :], ALU.mult))
                vop(lambda e, s, t=t, par=par: e.tensor_tensor(
                    out_scr[t % 2][:, :], s1_big[:, t, :], xo_blk[par][:, :], ALU.add),
                    inc=True)
                K.wait("gpsimd", "sV")
                dma("gpsimd", "qOUT%d" % (t % 2),
                    out[t * 128:(t + 1) * 128, ob * 512:(ob + 1) * 512],
                    out_scr[t % 2][:, :])
                snapf[par] = [("sV", K.count("sV"))]
                if t + 4 < TT:
                    opj_load(t + 4)
        K.wait("gpsimd", "qOUT0")
        K.wait("gpsimd", "qOUT1")
        K.wait("gpsimd", "qGF")
        K.wait("gpsimd", "qQG0")
        K.wait("gpsimd", "qQG1")
        K.wait("gpsimd", "qSC")

        sems = {n: stack.enter_context(nc.semaphore(n)) for n in sorted(K.sem_names)}
        block = stack.enter_context(nc.Block())

        @block.sync
        def _(e):
            K.replay("sync", e, sems)

        @block.scalar
        def _(e):
            K.replay("scalar", e, sems)

        @block.vector
        def _(e):
            K.replay("vector", e, sems)

        @block.tensor
        def _(e):
            K.replay("tensor", e, sems)

        @block.gpsimd
        def _(e):
            K.replay("gpsimd", e, sems)

    return nc


_NC_CACHE = [None]


def kernel(**inputs):
    x = np.asarray(inputs["L0_attn_hidden_states"], dtype=np.float32)
    wqkv = np.asarray(inputs["L0_attn_w_qkv"], dtype=np.float32)
    bqkv = np.asarray(inputs["L0_attn_b_qkv"], dtype=np.float32)
    w_out = np.asarray(inputs["L0_attn_w_out"], dtype=np.float32)
    b_out = np.asarray(inputs["L0_attn_b_out"], dtype=np.float32)
    w_gate = np.asarray(inputs["L0_ffn_w_gate"], dtype=np.float32)
    w_up = np.asarray(inputs["L0_ffn_w_up"], dtype=np.float32)
    w_down = np.asarray(inputs["L0_ffn_w_down"], dtype=np.float32)
    dperm = np.ascontiguousarray(
        np.asarray(inputs["L0_ffn_down_reorder_index"]).astype(np.int32))

    wq_a = np.ascontiguousarray(wqkv[0:H])
    wk_a = np.ascontiguousarray(wqkv[H:2 * H])
    wvt_a = np.ascontiguousarray(wqkv[2 * H:3 * H].T)
    wot_a = np.ascontiguousarray(w_out.T)
    bq_a = np.ascontiguousarray(bqkv[0:H])
    bv_a = np.ascontiguousarray(bqkv[2 * H:3 * H])
    bo_a = np.ascontiguousarray(b_out)
    wg_a = np.ascontiguousarray(w_gate)
    wu_a = np.ascontiguousarray(w_up)
    wd_a = np.ascontiguousarray(w_down)

    if _NC_CACHE[0] is None:
        _NC_CACHE[0] = build_program()
    nc = _NC_CACHE[0]

    in_maps = []
    for c in range(8):
        b, qb = c // 4, c % 4
        in_maps.append(dict(
            xb=np.ascontiguousarray(x[b]),
            xo=np.ascontiguousarray(x[b, qb * TO:(qb + 1) * TO]),
            wq=wq_a, wk=wk_a, wvt=wvt_a, wot=wot_a,
            bqv=bq_a, bvv=bv_a, bov=bo_a,
            wg=wg_a, wu=wu_a, wd=wd_a, pidx=dperm,
        ))
    res = run_bass_kernel_spmd(nc, in_maps, core_ids=list(range(8)))
    outp = np.empty((2, S, H), dtype=np.float32)
    for c in range(8):
        b, qb = c // 4, c % 4
        outp[b, qb * TO:(qb + 1) * TO] = np.asarray(res.results[c]["out"])
    return outp


# version 19
# speedup vs baseline: 1.0576x; 1.0396x over previous
import numpy as np
import ml_dtypes
import concourse.bass as bass
import concourse.mybir as mybir
from concourse.bass import IndirectOffsetOnAxis
from concourse.bass_utils import run_bass_kernel_spmd

F32 = mybir.dt.float32
BF16 = mybir.dt.bfloat16
FP8 = mybir.dt.float8e4
U32 = mybir.dt.uint32
I32 = mybir.dt.int32
AF = mybir.ActivationFunctionType
AX = mybir.AxisListType
ALU = mybir.AluOpType

H = 2048
HC = 16
S = 4096
STL = 32
TO = 1024
TT = 8
I = 5632
ICH = 44
IBL = 11
NMAC = 2
MTT = 4
QQ = 256
NQQ = 4
RSQH = 1.0 / float(np.sqrt(np.float32(2048.0)))
ENGS = ("sync", "scalar", "vector", "tensor", "gpsimd")
KBy = 1024


class KB:

    def __init__(self):
        self.ops = {e: [] for e in ENGS}
        self.cnt = {}
        self.sem_names = set()

    def count(self, sem):
        return self.cnt.get(sem, 0)

    def add(self, eng, fn, inc=None):
        if inc is not None:
            sem, n = inc
            self.sem_names.add(sem)
            self.cnt[sem] = self.cnt.get(sem, 0) + n

            def wrapped(e, sems, fn=fn, sem=sem, n=n):
                fn(e, sems).then_inc(sems[sem], n)
            self.ops[eng].append(wrapped)
        else:
            self.ops[eng].append(fn)

    def wait(self, eng, sem, val=None):
        self.sem_names.add(sem)
        if val is None:
            val = self.count(sem)
        if val <= 0:
            return

        def w(e, sems, sem=sem, val=val):
            e.wait_ge(sems[sem], val)
        self.ops[eng].append(w)

    def barrier(self):
        snap = dict(self.cnt)
        for eng in ENGS:
            for sem, val in snap.items():
                self.wait(eng, sem, val)

    def replay(self, eng_name, e, sems):
        for fn in self.ops[eng_name]:
            fn(e, sems)


def build_program():
    nc = bass.Bass("TRN2", target_bir_lowering=False)

    xb = nc.dram_tensor("xb", [S, H], F32, kind="ExternalInput").ap()
    xo = nc.dram_tensor("xo", [TO, H], F32, kind="ExternalInput").ap()
    wq = nc.dram_tensor("wq", [H, H], F32, kind="ExternalInput").ap()
    wk = nc.dram_tensor("wk", [H, H], F32, kind="ExternalInput").ap()
    wvt = nc.dram_tensor("wvt", [H, H], F32, kind="ExternalInput").ap()
    wot = nc.dram_tensor("wot", [H, H], F32, kind="ExternalInput").ap()
    bqv = nc.dram_tensor("bqv", [H], F32, kind="ExternalInput").ap()
    bvv = nc.dram_tensor("bvv", [H], F32, kind="ExternalInput").ap()
    bov = nc.dram_tensor("bov", [H], F32, kind="ExternalInput").ap()
    wg = nc.dram_tensor("wg", [I, H], F32, kind="ExternalInput").ap()
    wu = nc.dram_tensor("wu", [I, H], F32, kind="ExternalInput").ap()
    wd = nc.dram_tensor("wd", [H, I], F32, kind="ExternalInput").ap()
    pidx = nc.dram_tensor("pidx", [I], I32, kind="ExternalInput").ap()
    out = nc.dram_tensor("out", [TO, H], F32, kind="ExternalOutput").ap()

    d_xnb = nc.dram_tensor("d_xnb", [S, H], BF16).ap()
    d_wqb = nc.dram_tensor("d_wqb", [H, H], BF16).ap()
    d_qg = nc.dram_tensor("d_qg", [I, H], BF16).ap()
    d_qu = nc.dram_tensor("d_qu", [I, H], BF16).ap()
    d_wdq = nc.dram_tensor("d_wdq", [H, I], BF16).ap()
    d_wdt = nc.dram_tensor("d_wdt", [I, H], BF16).ap()
    d_qt = nc.dram_tensor("d_qt", [512, I], BF16).ap()
    d_z = nc.dram_tensor("d_z", [HC, 128, TO], BF16).ap()
    d_gf = nc.dram_tensor("d_gf", [TT, 128, H], BF16).ap()
    d_dbf = nc.dram_tensor("d_dbf", [1, S], BF16).ap()
    d_xnf = nc.dram_tensor("d_xnf", [8, 128, HC * 512], BF16).ap()
    d_wsg = nc.dram_tensor("d_wsg", [1, I], F32).ap()
    d_wsu = nc.dram_tensor("d_wsu", [1, I], F32).ap()
    d_wsd = nc.dram_tensor("d_wsd", [1, H], F32).ap()

    K = KB()
    from contextlib import ExitStack
    stack = ExitStack()
    with stack:
        AR = 204 * KBy
        arena = stack.enter_context(nc.sbuf_tensor("arena", [128, AR], mybir.dt.uint8))

        def view(off, dtype, shape):
            n = int(np.prod(shape[1:]))
            nb = n * mybir.dt.size(dtype)
            assert off % 64 == 0 and off + nb <= AR, (off, nb)
            v = arena[:, off:off + nb].bitcast(dtype)
            if len(shape) == 2:
                return v
            return v.rearrange("p (m q) -> p m q", q=shape[2])

        ones_bf = view(0, BF16, [128, 128])[0:1, :]
        pcols = view(256, U32, [128, ICH])
        bq_cols = view(448, BF16, [128, HC])
        bv_cols = view(512, F32, [128, HC])
        wbar_cols = view(576, BF16, [128, HC])
        sx_cols = view(640, F32, [128, TT])
        st_cols = view(704, F32, [128, TT])
        smalls = view(768, F32, [128, 64])
        (SUM, SSQ, MEAN, RSTD, VARC, RMX, RCP, C2, MTMP, MCOL0, MCOL1, NEGM0,
         NEGM1, LCOL0, LCOL1, LINV0, LINV1, TMPA, TMPB,
         RMX1, RCP1, C21, TMPA1, SUM1, SSQ1, MEAN1, RSTD1, VARC1, TMPB1,
         RMX2, RCP2, C22, TMPA2, RMX3, RCP3, C23, TMPA3,
         RMX4, RCP4, C24, TMPA4, RMX5, RCP5, C25, TMPA5,
         RMX6, RCP6, C26, TMPA6, RMX7, RCP7, C27, TMPA7) = range(53)

        def col(i):
            return smalls[:, i:i + 1]

        B = 2 * KBy

        wk_bf = view(B, BF16, [128, HC, H])
        A_sb = view(B + 64 * KBy, BF16, [128, HC, H])
        wq_cb = [view(B + 130 * KBy + i * 4 * KBy, BF16, [128, HC, 128]) for i in range(2)]

        xt_bf = [view(B + 138 * KBy + i * 4 * KBy, BF16, [128, H]) for i in range(2)]
        sq_scr = view(B + 146 * KBy, BF16, [128, H])
        disc_bf = view(B + 150 * KBy, BF16, [128, H])
        xt_f32 = [view(B + 154 * KBy + i * 8 * KBy, F32, [128, H]) for i in range(2)]
        wk_f32 = [view(B + 170 * KBy + i * 8 * KBy, F32, [128, H]) for i in range(2)]

        xo_f32 = [view(B + i * 8 * KBy, F32, [128, H]) for i in range(2)]
        xn_scr = view(B + 16 * KBy, F32, [128, H])
        sq3_scr = view(B + 24 * KBy, BF16, [128, H])
        qx8_tm = view(B + 28 * KBy, FP8, [128, H])
        qx_bf_tm = view(B + 30 * KBy, BF16, [128, H])
        tb_scr = view(B + 34 * KBy, BF16, [128, HC, 128])
        z_scr = [view(B + 38 * KBy + i * KBy, BF16, [128, 512]) for i in range(2)]
        disc3 = view(B + 40 * KBy, BF16, [128, H])
        xn_own_fm = view(B + 128 * KBy, BF16, [128, HC, TO])
        qx_fm8 = view(B + 160 * KBy, FP8, [128, HC, TO])

        wt_f32 = ([view(B + 44 * KBy + i * 8 * KBy, F32, [128, H]) for i in range(2)]
                  + [view(B + 176 * KBy + i * 8 * KBy, F32, [128, H]) for i in range(2)])
        q8_scr = ([view(B + 60 * KBy + i * 2 * KBy, FP8, [128, H]) for i in range(2)]
                  + [view(B + 192 * KBy + i * 2 * KBy, FP8, [128, H]) for i in range(2)])
        wdp_f32 = [view(B, F32, [128, I]), view(B + 176 * KBy, F32, [128, I])]
        q8_big = [view(B + 22 * KBy + i * 6 * KBy, FP8, [128, I]) for i in range(2)]
        wdt_scr = [view(B + 152 * KBy + i * 4 * KBy, BF16, [128, H]) for i in range(2)]

        t2v = view(B, F32, [128, MTT, I])
        g_rhsb = [view(B + 88 * KBy + i * 16 * KBy, BF16, [128, HC, 512]) for i in range(2)]
        u_rhsb = [view(B + 120 * KBy + i * 16 * KBy, BF16, [128, HC, 512]) for i in range(2)]
        qt_fm = view(B + 88 * KBy, BF16, [128, ICH, 512])
        qt8_tm = view(B + 152 * KBy, FP8, [128, I])
        wd_rhs = [view(B + 188 * KBy + i * 2 * KBy, BF16, [128, 1024]) for i in range(2)]
        wsg_bc = [view(B + 176 * KBy + i * 2 * KBy, F32, [128, 512]) for i in range(2)]
        wsu_bc = [view(B + 180 * KBy + i * 2 * KBy, F32, [128, 512]) for i in range(2)]
        wsd_bc = [view(B + 184 * KBy + i * 2 * KBy, F32, [128, 512]) for i in range(2)]
        g1_scr = view(B + 188 * KBy, F32, [128, 512])
        sg_scr = view(B + 190 * KBy, F32, [128, 512])
        u1_scr = view(B + 192 * KBy, F32, [128, 512])
        y1_scr = view(B + 194 * KBy, F32, [128, 512])
        gf_scr = view(B + 196 * KBy, BF16, [128, 512])

        scores = [view(B + i * 16 * KBy, F32, [128, S]) for i in range(2)]
        p_bf = [view(B + 32 * KBy + i * 8 * KBy, BF16, [128, S]) for i in range(2)]
        pT = view(B + 48 * KBy, BF16, [128, 32, QQ])
        fm_dest = [view(B + 64 * KBy + i * 16 * KBy, BF16, [128, HC, 512]) for i in range(2)]
        px_fm = view(B + 96 * KBy, BF16, [128, HC, TO])
        ctx_fm = view(B + 128 * KBy, BF16, [128, HC, TO])
        px_scr = view(B + 160 * KBy, BF16, [128, H])
        z_sb = view(B + 164 * KBy, BF16, [128, HC, QQ])
        xnb_tile = [view(B + 172 * KBy + i * 4 * KBy, BF16, [128, H]) for i in range(4)]
        d_row_bf = view(B + 188 * KBy, BF16, [1, 512])
        d_full = view(B + 189 * KBy, BF16, [1, S])
        wbig = view(B, BF16, [128, HC, H])
        bo_bc = view(B + 184 * KBy, F32, [128, 512])
        s1_big = view(B + 64 * KBy, F32, [128, TT, 512])
        gf_tile = [view(B + 80 * KBy + i * KBy, BF16, [128, 512]) for i in range(4)]
        xo_blk = [view(B + 84 * KBy + i * 2 * KBy, F32, [128, 512]) for i in range(4)]
        out_scr = [view(B + 92 * KBy + i * 2 * KBy, F32, [128, 512]) for i in range(2)]

        PB = [stack.enter_context(nc.psum_tensor("pb%d" % i, [128, 512], F32))
              for i in range(8)]
        def dma(eng, sem, out_ap, in_ap, **kw):
            K.add(eng, lambda e, s, o=out_ap, i=in_ap, kw=kw:
                  e.dma_start(out=o, in_=i, **kw), inc=(sem, 16))

        def dmat(sem, out_ap, in_ap, eng="sync"):
            K.add(eng, lambda e, s, o=out_ap, i=in_ap:
                  e.dma_start_transpose(out=o, in_=i), inc=(sem, 16))

        def mm(lhsT, rhs, pout, start, stop, inc=False):
            K.add("tensor",
                  lambda e, s, l=lhsT, r=rhs, p=pout, st=start, sp=stop:
                  e.matmul(p, l, r, start=st, stop=sp),
                  inc=("sT", 1) if inc else None)

        def vop(fn, inc=False):
            K.wait("vector", "sV")
            K.add("vector", fn, inc=("sV", 1))

        def aop(fn, inc=False):
            K.wait("scalar", "sC")
            K.add("scalar", fn, inc=("sC", 1))

        def gop(sem, fn):
            K.add("gpsimd", fn, inc=(sem, 16))

        def dram_col(dram_row_ap):
            return dram_row_ap.rearrange("o (p q) -> p (o q)", q=1)

        vop(lambda e, s: e.memset(ones_bf, 1.0))
        for c in range(HC):
            dma("gpsimd", "qCW", d_wqb[c * 128:(c + 1) * 128, :],
                wq[c * 128:(c + 1) * 128, :])
        for j in range(ICH):
            dma("gpsimd", "qPC", pcols[:, j:j + 1],
                pidx[j * 128:(j + 1) * 128].rearrange("(p o) -> p o", o=1).bitcast(U32))
        for c in range(HC):
            dma("gpsimd", "qPC", bq_cols[:, c:c + 1],
                bqv[c * 128:(c + 1) * 128].rearrange("(p o) -> p o", o=1))
            dma("gpsimd", "qPC", bv_cols[:, c:c + 1],
                bvv[c * 128:(c + 1) * 128].rearrange("(p o) -> p o", o=1))

        def gen_p1():
            snapw = {}
            for c in range(HC):
                par = c % 2
                if par in snapw:
                    K.wait("sync", "sV", snapw[par])
                dma("sync", "qWK%d" % par, wk_f32[par], wk[c * 128:(c + 1) * 128, :])
                K.wait("vector", "qWK%d" % par)
                vop(lambda e, s, c=c, par=par:
                    e.tensor_copy(wk_bf[:, c, :], wk_f32[par][:, :]), inc=True)
                snapw[par] = K.count("sV")
            wk_done = K.count("sV")
            K.wait("sync", "qCW")
            bank = 0
            for hi in range(HC):
                cb = wq_cb[hi % 2]
                qn = "qWQ%d" % (hi % 2)
                if hi >= 2:
                    K.wait("sync", "sT")
                for o in range(HC):
                    dma("sync", qn, cb[:, o, :],
                        d_wqb[o * 128:(o + 1) * 128, hi * 128:(hi + 1) * 128])
                K.wait("tensor", qn)
                K.wait("tensor", "sV", wk_done)
                for hob in range(4):
                    pb = PB[bank % 8]
                    if bank >= 8:
                        K.wait("tensor", "sV")
                    for o in range(HC):
                        mm(cb[:, o, :], wk_bf[:, o, hob * 512:(hob + 1) * 512],
                           pb[:, :], start=(o == 0), stop=(o == HC - 1),
                           inc=(o == HC - 1))
                    K.wait("vector", "sT")
                    vop(lambda e, s, pb=pb, hi=hi, hob=hob:
                        e.tensor_copy(A_sb[:, hi, hob * 512:(hob + 1) * 512], pb[:, :]),
                        inc=True)
                    bank += 1
            K.wait("tensor", "qPC")
            for hob in range(4):
                pb = PB[bank % 8]
                K.wait("tensor", "sV")
                for o in range(HC):
                    mm(bq_cols[:, o:o + 1], wk_bf[:, o, hob * 512:(hob + 1) * 512],
                       pb[0:1, :], start=(o == 0), stop=(o == HC - 1), inc=(o == HC - 1))
                K.wait("vector", "sT")
                vop(lambda e, s, pb=pb: e.tensor_scalar(
                    d_row_bf[0:1, :], pb[0:1, :], RSQH, None, ALU.mult), inc=True)
                K.wait("gpsimd", "sV")
                dma("gpsimd", "qDBF", d_dbf[0:1, hob * 512:(hob + 1) * 512], d_row_bf[0:1, :])
                K.wait("vector", "qDBF")
                bank += 1
                yield
            K.wait("gpsimd", "qDBF")
            for c in range(HC):
                dma("gpsimd", "qPC", wbar_cols[:, c:c + 1],
                    dram_col(d_dbf[0:1, c * 128:(c + 1) * 128]))

        def gen_p2():
            snap2 = {}
            for t in range(STL):
                par = t % 2
                buf = xt_f32[par]
                obuf = xt_bf[par]
                qn = "qX2%d" % par
                cS, cQ, cM, cV, cR, cT = ((SUM, SSQ, MEAN, VARC, RSTD, TMPB) if par == 0
                                          else (SUM1, SSQ1, MEAN1, VARC1, RSTD1, TMPB1))
                cA = TMPA if par == 0 else TMPA1
                if t >= 2:
                    for sem, val in snap2[par]["ld"]:
                        K.wait("gpsimd", sem, val)
                dma("gpsimd", qn, buf, xb[t * 128:(t + 1) * 128, :])
                K.wait("scalar", qn)
                aop(lambda e, s, b=buf, cQ=cQ: e.activation(sq_scr[:, :], b, AF.Square,
                                                            accum_out=col(cQ)))
                K.wait("vector", qn)
                vop(lambda e, s, b=buf, cS=cS, par=par: e.tensor_scalar(
                    disc_bf[:, :], b, 1.0, 0.0, ALU.mult, ALU.add, accum_out=col(cS)))
                K.wait("vector", "sC")
                vop(lambda e, s, cM=cM, cS=cS: e.tensor_scalar(col(cM), col(cS), 1.0 / H, None, ALU.mult))
                vop(lambda e, s, cV=cV, cQ=cQ: e.tensor_scalar(col(cV), col(cQ), 1.0 / H, None, ALU.mult))
                vop(lambda e, s, cA=cA, cM=cM: e.tensor_tensor(col(cA), col(cM), col(cM), ALU.mult))
                vop(lambda e, s, cV=cV, cA=cA: e.tensor_tensor(col(cV), col(cV), col(cA), ALU.subtract))
                vop(lambda e, s, cV=cV: e.tensor_scalar(col(cV), col(cV), 1e-5, None, ALU.add),
                    inc=True)
                K.wait("scalar", "sV")
                aop(lambda e, s, cT=cT, cV=cV: e.activation(col(cT), col(cV), AF.Sqrt))
                K.wait("vector", "sC")
                vop(lambda e, s, cR=cR, cT=cT: e.reciprocal(col(cR), col(cT)))
                if t >= 2:
                    K.wait("vector", "qNB%d" % par, snap2[par]["st"])
                vop(lambda e, s, b=buf, ob=obuf, cM=cM, cR=cR: e.tensor_scalar(
                    ob, b, col(cM), col(cR), ALU.subtract, ALU.mult), inc=True)
                K.wait("scalar", "sV")
                dma("scalar", "qNB%d" % par, d_xnb[t * 128:(t + 1) * 128, :], obuf)
                snap2[par] = {"ld": [("sV", K.count("sV")), ("sC", K.count("sC"))],
                              "st": K.count("qNB%d" % par)}
                yield

        def gen_p3():
            snap3 = {}
            for t in range(TT):
                par = t % 2
                buf = xo_f32[par]
                qn = "qX3%d" % par
                cS, cQ, cM, cV, cR, cT = ((SUM, SSQ, MEAN, VARC, RSTD, TMPB) if par == 0
                                          else (SUM1, SSQ1, MEAN1, VARC1, RSTD1, TMPB1))
                cA = TMPA if par == 0 else TMPA1
                rmx, rcp, c2 = (RMX, RCP, C2) if par == 0 else (RMX1, RCP1, C21)
                if t >= 2:
                    for sem, val in snap3[par]:
                        K.wait("gpsimd", sem, val)
                dma("gpsimd", qn, buf, xo[t * 128:(t + 1) * 128, :])
                K.wait("scalar", qn)
                aop(lambda e, s, b=buf, cQ=cQ: e.activation(sq3_scr[:, :], b, AF.Square,
                                                            accum_out=col(cQ)))
                K.wait("vector", qn)
                vop(lambda e, s, b=buf, cS=cS: e.tensor_scalar(
                    disc3[:, :], b, 1.0, 0.0, ALU.mult, ALU.add, accum_out=col(cS)))
                K.wait("vector", "sC")
                vop(lambda e, s, cM=cM, cS=cS: e.tensor_scalar(col(cM), col(cS), 1.0 / H, None, ALU.mult))
                vop(lambda e, s, cV=cV, cQ=cQ: e.tensor_scalar(col(cV), col(cQ), 1.0 / H, None, ALU.mult))
                vop(lambda e, s, cA=cA, cM=cM: e.tensor_tensor(col(cA), col(cM), col(cM), ALU.mult))
                vop(lambda e, s, cV=cV, cA=cA: e.tensor_tensor(col(cV), col(cV), col(cA), ALU.subtract))
                vop(lambda e, s, cV=cV: e.tensor_scalar(col(cV), col(cV), 1e-5, None, ALU.add),
                    inc=True)
                K.wait("scalar", "sV")
                aop(lambda e, s, cT=cT, cV=cV: e.activation(col(cT), col(cV), AF.Sqrt))
                K.wait("vector", "sC")
                vop(lambda e, s, cR=cR, cT=cT: e.reciprocal(col(cR), col(cT)))
                K.wait("vector", "qFM")
                K.wait("vector", "qTB")
                K.wait("vector", "sC")
                vop(lambda e, s, b=buf, cM=cM, cR=cR: e.tensor_scalar(
                    xn_scr[:, :], b, col(cM), col(cR), ALU.subtract, ALU.mult))
                vop(lambda e, s: e.tensor_copy(qx_bf_tm[:, :], xn_scr[:, :]))
                vop(lambda e, s, rmx=rmx: e.reduce_max(col(rmx), xn_scr[:, :], AX.X,
                                                       apply_absolute_value=True))
                vop(lambda e, s, rmx=rmx, rcp=rcp: e.reciprocal(col(rcp), col(rmx)))
                vop(lambda e, s, rcp=rcp, c2=c2: e.tensor_scalar(col(c2), col(rcp), 224.0,
                                                                 None, ALU.mult))
                vop(lambda e, s, t=t, rmx=rmx: e.tensor_scalar(
                    sx_cols[:, t:t + 1], col(rmx), 1.0 / 224.0, None, ALU.mult), inc=True)
                K.wait("sync", "sV")
                dmat("qFM", xn_own_fm[:, :, t * 128:(t + 1) * 128], qx_bf_tm[:, :])
                K.wait("scalar", "sV")
                aop(lambda e, s, c2=c2: e.activation(qx8_tm[:, :], xn_scr[:, :], AF.Copy,
                                                     scale=col(c2)))
                K.wait("vector", "sC")
                K.wait("vector", "qFM")
                vop(lambda e, s: e.tensor_copy(qx_bf_tm[:, :], qx8_tm[:, :]), inc=True)
                K.wait("sync", "sV")
                K.wait("sync", "sC")
                dmat("qTB", tb_scr[:, :, :], qx_bf_tm[:, :])
                K.wait("scalar", "qTB")
                aop(lambda e, s, t=t: e.activation(
                    qx_fm8[:, :, t * 128:(t + 1) * 128], tb_scr[:, :, :], AF.Copy))
                snap3[par] = [("sV", K.count("sV")), ("sC", K.count("sC"))]
                yield

        def gen_p35():
            K.wait("tensor", "qFM")
            snapz = {}
            bank = 0
            for ho in range(HC):
                for qb in range(2):
                    pb = PB[bank % 8]
                    par = bank % 2
                    if bank >= 8:
                        K.wait("tensor", "sC")
                    for hi in range(HC):
                        mm(A_sb[:, hi, ho * 128:(ho + 1) * 128],
                           xn_own_fm[:, hi, qb * 512:(qb + 1) * 512],
                           pb[:, :], start=(hi == 0), stop=(hi == HC - 1),
                           inc=(hi == HC - 1))
                    K.wait("scalar", "sT")
                    if par in snapz:
                        for sem, val in snapz[par]:
                            K.wait("scalar", sem, val)
                    aop(lambda e, s, pb=pb, par=par: e.activation(
                        z_scr[par][:, :], pb[:, :], AF.Copy, scale=RSQH))
                    K.wait("gpsimd", "sC")
                    dma("gpsimd", "qZS%d" % par, d_z[ho, :, qb * 512:(qb + 1) * 512],
                        z_scr[par][:, :])
                    snapz[par] = [("qZS%d" % par, K.count("qZS%d" % par))]
                    bank += 1
                    yield

        def gen_p4a():
            cols = [(RMX2, RCP2, C22, TMPA2), (RMX3, RCP3, C23, TMPA3),
                    (RMX4, RCP4, C24, TMPA4), (RMX5, RCP5, C25, TMPA5)]
            snapa = {}
            for wi, (wsrc, qdst, wsdst) in enumerate(((wg, d_qg, d_wsg), (wu, d_qu, d_wsu))):
                for j in range(ICH):
                    jj = wi * ICH + j
                    par = jj % 4
                    buf = wt_f32[par]
                    qn = "qW4%d" % par
                    rmx, rcp, c2, tmpa = cols[par]
                    if par in snapa:
                        for sem, val in snapa[par]["ld"]:
                            K.wait("sync", sem, val)
                    dma("sync", qn, buf, wsrc[j * 128:(j + 1) * 128, :])
                    K.wait("vector", qn)
                    if par in snapa:
                        K.wait("vector", "qWSA%d" % par, snapa[par]["ws"])
                    vop(lambda e, s, b=buf, rmx=rmx: e.reduce_max(
                        col(rmx), b, AX.X, apply_absolute_value=True))
                    vop(lambda e, s, rmx=rmx, rcp=rcp: e.reciprocal(col(rcp), col(rmx)))
                    vop(lambda e, s, rcp=rcp, c2=c2: e.tensor_scalar(
                        col(c2), col(rcp), 224.0, None, ALU.mult))
                    vop(lambda e, s, rmx=rmx, tmpa=tmpa: e.tensor_scalar(
                        col(tmpa), col(rmx), 1.0 / 224.0, None, ALU.mult), inc=True)
                    K.wait("gpsimd", "sV")
                    dma("gpsimd", "qWSA%d" % par, dram_col(wsdst[0:1, j * 128:(j + 1) * 128]),
                        col(tmpa))
                    K.wait("scalar", "sV")
                    K.wait("scalar", "qQG%d" % par)
                    aop(lambda e, s, b=buf, c2=c2, par=par: e.activation(
                        q8_scr[par][:, :], b, AF.Copy, scale=col(c2)))
                    K.wait("gpsimd", "sC")
                    dma("gpsimd", "qQG%d" % par, qdst[j * 128:(j + 1) * 128, :],
                        q8_scr[par][:, :])
                    snapa[par] = {"ld": [("sV", K.count("sV")), ("sC", K.count("sC"))],
                                  "ws": K.count("qWSA%d" % par)}
                    yield
        def gen_p4b():
            for sem in ("sV", "sC", "qTB", "qFM", "qQG2", "qQG3"):
                K.wait("sync", sem)
                K.wait("scalar", sem)
            snapb = {}
            cols = [(RMX, RCP, C2, TMPA), (RMX1, RCP1, C21, TMPA1)]
            for j in range(HC):
                par = j % 2
                buf = wdp_f32[par]
                qn = "qW5%d" % par
                rmx, rcp, c2, tmpa = cols[par]
                if par in snapb:
                    for sem, val in snapb[par]["ld"]:
                        K.wait("sync", sem, val)
                dma("sync", qn, buf, wd[j * 128:(j + 1) * 128, :])
                K.wait("vector", qn)
                if par in snapb:
                    K.wait("vector", "qWSA%d" % par, snapb[par]["ws"])
                vop(lambda e, s, b=buf, rmx=rmx: e.reduce_max(
                    col(rmx), b, AX.X, apply_absolute_value=True))
                vop(lambda e, s, rmx=rmx, rcp=rcp: e.reciprocal(col(rcp), col(rmx)))
                vop(lambda e, s, rcp=rcp, c2=c2: e.tensor_scalar(
                    col(c2), col(rcp), 224.0, None, ALU.mult))
                vop(lambda e, s, rmx=rmx, tmpa=tmpa: e.tensor_scalar(
                    col(tmpa), col(rmx), 1.0 / 224.0, None, ALU.mult), inc=True)
                K.wait("gpsimd", "sV")
                dma("gpsimd", "qWSA%d" % par, dram_col(d_wsd[0:1, j * 128:(j + 1) * 128]),
                    col(tmpa))
                K.wait("scalar", "sV")
                K.wait("scalar", "qQG%d" % par)
                aop(lambda e, s, b=buf, c2=c2, par=par: e.activation(
                    q8_big[par][:, :], b, AF.Copy, scale=col(c2)))
                K.wait("gpsimd", "sC")
                dma("gpsimd", "qQG%d" % par, d_wdq[j * 128:(j + 1) * 128, :],
                    q8_big[par][:, :])
                snapb[par] = {"ld": [("sV", K.count("sV")), ("sC", K.count("sC"))],
                              "ws": K.count("qWSA%d" % par)}
                yield
        def gen_p4c():
            for j in range(ICH):
                buf = wdt_scr[j % 2]
                qn = "qT4%d" % (j % 2)
                if j >= 2:
                    K.wait("sync", "qSC")
                dmat(qn, buf, d_wdq[:, j * 128:(j + 1) * 128])
                K.wait("gpsimd", qn)
                gop("qSC", lambda e, s, b=buf, j=j: e.indirect_dma_start(
                    out=d_wdt, out_offset=IndirectOffsetOnAxis(ap=pcols[:, j:j + 1], axis=0),
                    in_=b, in_offset=None))
                yield

        def gen_p4d():
            snapg = {}
            for mac in range(NMAC):
                tts = [mac * MTT + i for i in range(MTT)]
                for ib in range(IBL):
                    gib = mac * IBL + ib
                    par = gib % 2
                    K.wait("gpsimd", "sV")
                    dma("gpsimd", "qWS", wsg_bc[ib % 2],
                        d_wsg[0:1, ib * 512:(ib + 1) * 512].broadcast_to([128, 512]))
                    dma("gpsimd", "qWS", wsu_bc[ib % 2],
                        d_wsu[0:1, ib * 512:(ib + 1) * 512].broadcast_to([128, 512]))
                    qn = "qGR%d" % par
                    if ib == 0 and mac > 0:
                        K.wait("sync", "sT")
                    if par in snapg:
                        for sem, val in snapg[par]:
                            K.wait("sync", sem, val)
                    dmat(qn, g_rhsb[par][:, :, :], d_qg[ib * 512:(ib + 1) * 512, :])
                    dmat(qn, u_rhsb[par][:, :, :], d_qu[ib * 512:(ib + 1) * 512, :])
                    K.wait("tensor", qn)
                    for half in range(2):
                        hts = tts[half * 2:(half + 1) * 2]
                        hkey = "gh%d" % half
                        pbase = half * 4
                        if hkey in snapg:
                            K.wait("tensor", "sV", snapg[hkey])
                        for hc in range(HC):
                            for i_h, t in enumerate(hts):
                                mm(qx_fm8[:, hc, t * 128:(t + 1) * 128], g_rhsb[par][:, hc, :],
                                   PB[pbase + i_h][:, :], start=(hc == 0), stop=(hc == HC - 1))
                                mm(qx_fm8[:, hc, t * 128:(t + 1) * 128], u_rhsb[par][:, hc, :],
                                   PB[pbase + 2 + i_h][:, :], start=(hc == 0),
                                   stop=(hc == HC - 1),
                                   inc=(hc == HC - 1 and i_h == 1))
                        snapg[par] = [("sT", K.count("sT"))]
                        K.wait("vector", "sT")
                        K.wait("vector", "qWS")
                        for i_h, t in enumerate(hts):
                            vop(lambda e, s, pbase=pbase, i_h=i_h, t=t, ib=ib:
                                e.scalar_tensor_tensor(
                                    g1_scr[:, :], PB[pbase + i_h][:, :], sx_cols[:, t:t + 1],
                                    wsg_bc[ib % 2], ALU.mult, ALU.mult), inc=True)
                            K.wait("scalar", "sV")
                            aop(lambda e, s: e.activation(sg_scr[:, :], g1_scr[:, :], AF.Silu))
                            vop(lambda e, s, pbase=pbase, i_h=i_h, t=t, ib=ib:
                                e.scalar_tensor_tensor(
                                    u1_scr[:, :], PB[pbase + 2 + i_h][:, :],
                                    sx_cols[:, t:t + 1], wsu_bc[ib % 2],
                                    ALU.mult, ALU.mult))
                            K.wait("vector", "sC")
                            it_g = half * 2 + i_h
                            vop(lambda e, s, it_g=it_g, ib=ib: e.tensor_tensor(
                                t2v[:, it_g, ib * 512:(ib + 1) * 512], sg_scr[:, :],
                                u1_scr[:, :], ALU.mult), inc=True)
                        snapg[hkey] = K.count("sV")
                        yield
                for qi in range(4):
                    K.wait("gpsimd", "qQTF%d" % qi)
                for i_t, t in enumerate(tts):
                    vop(lambda e, s, i_t=i_t: e.reduce_max(
                        col(RMX), t2v[:, i_t, :], AX.X, apply_absolute_value=True), inc=True)
                    K.wait("vector", "sC")
                    vop(lambda e, s: e.reciprocal(col(RCP), col(RMX)))
                    vop(lambda e, s: e.tensor_scalar(col(C2), col(RCP), 224.0, None, ALU.mult))
                    vop(lambda e, s, t=t: e.tensor_scalar(
                        st_cols[:, t:t + 1], col(RMX), 1.0 / 224.0, None, ALU.mult), inc=True)
                    K.wait("scalar", "sV")
                    K.wait("scalar", "qQT")
                    K.wait("scalar", "qSC")
                    aop(lambda e, s, i_t=i_t: e.activation(qt8_tm[:, :], t2v[:, i_t, :],
                                                           AF.Copy, scale=col(C2)), inc=True)
                    K.wait("gpsimd", "sC")
                    dma("gpsimd", "qQT", d_qt[i_t * 128:(i_t + 1) * 128, :], qt8_tm[:, :])
                    yield
                K.wait("sync", "qQT")
                K.wait("sync", "sT")
                for ic in range(ICH):
                    dmat("qQTF%d" % (ic % 4), qt_fm[:, ic, :],
                         d_qt[:, ic * 128:(ic + 1) * 128])
                K.wait("sync", "qSC")
                K.wait("scalar", "qSC")
                K.wait("sync", "sV")
                qtf_snap = [K.count("qQTF%d" % (ic % 4)) for ic in range(ICH)]
                for hp in range(2):
                    hbs = (2 * hp, 2 * hp + 1)
                    K.wait("gpsimd", "sV")
                    for bi, hb in enumerate(hbs):
                        dma("gpsimd", "qWS", wsd_bc[bi],
                            d_wsd[0:1, hb * 512:(hb + 1) * 512].broadcast_to([128, 512]))
                    if hp == 1 or mac > 0:
                        K.wait("tensor", "sV")
                    for ic in range(ICH):
                        buf = wd_rhs[ic % 2]
                        qn = "qWD%d" % (ic % 2)
                        if ic >= 2 or hp == 1 or mac > 0:
                            K.wait("sync", "sT")
                        dma("sync", qn, buf,
                            d_wdt[ic * 128:(ic + 1) * 128,
                                  2 * hp * 512:(2 * hp + 2) * 512])
                        K.wait("tensor", qn)
                        K.wait("tensor", "qQTF%d" % (ic % 4), qtf_snap[ic])
                        for bi, hb in enumerate(hbs):
                            for i_t, t in enumerate(tts):
                                mm(qt_fm[:, ic, i_t * 128:(i_t + 1) * 128],
                                   buf[:, bi * 512:(bi + 1) * 512],
                                   PB[bi * 4 + i_t][:, :],
                                   start=(ic == 0), stop=(ic == ICH - 1),
                                   inc=(bi == 1 and i_t == MTT - 1))
                    K.wait("vector", "sT")
                    K.wait("vector", "qWS")
                    for bi, hb in enumerate(hbs):
                        for i_t, t in enumerate(tts):
                            vop(lambda e, s, bi=bi, i_t=i_t, t=t: e.scalar_tensor_tensor(
                                y1_scr[:, :], PB[bi * 4 + i_t][:, :], st_cols[:, t:t + 1],
                                wsd_bc[bi], ALU.mult, ALU.mult), inc=True)
                            K.wait("scalar", "sV")
                            K.wait("scalar", "qGF")
                            aop(lambda e, s: e.activation(gf_scr[:, :], y1_scr[:, :],
                                                          AF.Silu))
                            K.wait("gpsimd", "sC")
                            dma("gpsimd", "qGF", d_gf[t, :, hb * 512:(hb + 1) * 512],
                                gf_scr[:, :])
                            K.wait("vector", "sC")
                    yield
                    snapg["gh0"] = K.count("sV")
                    snapg["gh1"] = K.count("sV")
                    yield


        def ilv(*gens):
            gs = [(g, 1) if not isinstance(g, tuple) else g for g in gens]
            gs = [[g, w] for g, w in gs]
            while gs:
                for item in list(gs):
                    g, w = item
                    for _ in range(w):
                        try:
                            next(g)
                        except StopIteration:
                            gs.remove(item)
                            break

        ilv(gen_p1(), gen_p2())
        K.barrier()
        def chain(*gens):
            for g in gens:
                for _ in g:
                    yield
        ilv(chain(gen_p3(), gen_p35()), chain(gen_p4a(), gen_p4b()))
        K.barrier()
        ilv(gen_p4d(), (gen_p4c(), 4))
        K.barrier()
        snap5 = {}
        snapx = {}
        for qq in range(NQQ):
            K.wait("scalar", "sT")
            K.wait("scalar", "sC")
            for ho in range(HC):
                dma("scalar", "qZL", z_sb[:, ho, :], d_z[ho, :, qq * QQ:(qq + 1) * QQ])
            if qq > 0:
                K.wait("scalar", "qDBF")
                dma("scalar", "qDR", d_full[0:1, :], d_dbf[0:1, :])
            vop(lambda e, s: e.memset(col(MCOL0), -1e30))
            vop(lambda e, s: e.memset(col(MCOL1), -1e30))
            K.wait("vector", "sC")
            K.wait("tensor", "qZL")
            K.wait("tensor", "qDR")
            K.wait("vector", "qZL")
            for kb in range(8):
                par = kb % 2
                fmb = fm_dest[par]
                qn = "qFM%d" % par
                if par in snap5:
                    for sem, val in snap5[par]:
                        K.wait("sync", sem, val)
                if qq == 0:
                    dmat(qn, fmb, d_xnb[kb * 512:(kb + 1) * 512, :])
                    K.wait("gpsimd", qn)
                    dma("gpsimd", "qXF%d" % par, d_xnf[kb, :, :],
                        fmb[:, :, :].rearrange("p m q -> p (m q)"))
                else:
                    dma("sync", qn, fmb[:, :, :].rearrange("p m q -> p (m q)"),
                        d_xnf[kb, :, :])
                if qq == 0:
                    K.wait("tensor", qn)
                    for hc in range(HC):
                        mm(wbar_cols[:, hc:hc + 1], fmb[:, hc, :], PB[4][0:1, :],
                           start=(hc == 0), stop=(hc == HC - 1), inc=(hc == HC - 1))
                    K.wait("vector", "sT")
                    K.wait("vector", "qDBF")
                    vop(lambda e, s: e.tensor_copy(d_row_bf[0:1, :], PB[4][0:1, :]),
                        inc=True)
                    K.wait("gpsimd", "sV")
                    dma("gpsimd", "qDBF", d_dbf[0:1, kb * 512:(kb + 1) * 512],
                        d_row_bf[0:1, :])
                K.wait("tensor", qn)
                if qq == 0:
                    K.wait("tensor", "sV")
                elif kb >= 2:
                    for sem, val in snapsc[par]:
                        K.wait("tensor", sem, val)
                drow = d_row_bf[0:1, :] if qq == 0 else d_full[0:1, kb * 512:(kb + 1) * 512]
                for qt in range(2):
                    pb = PB[2 * par + qt]
                    for hc in range(HC):
                        mm(z_sb[:, hc, qt * 128:(qt + 1) * 128], fmb[:, hc, :],
                           pb[:, :], start=(hc == 0), stop=False)
                    mm(ones_bf, drow, pb[:, :], start=False, stop=True, inc=True)
                snap5[par] = ([("sT", K.count("sT"))] +
                              ([("qXF%d" % par, K.count("qXF%d" % par))] if qq == 0 else []))
                K.wait("vector", "sT")
                for qt in range(2):
                    pb = PB[2 * par + qt]
                    vop(lambda e, s, pb=pb, qt=qt, kb=kb: e.tensor_scalar(
                        scores[qt][:, kb * 512:(kb + 1) * 512], pb[:, :], 1.0, -1e30,
                        ALU.mult, ALU.max, accum_out=col(MTMP)), inc=True)
                    K.wait("vector", "sV")
                    vop(lambda e, s, qt=qt: e.tensor_tensor(
                        col(MCOL0 if qt == 0 else MCOL1), col(MCOL0 if qt == 0 else MCOL1),
                        col(MTMP), ALU.max), inc=True)
                snapsc = getattr(K, "_snapsc", {})
                snapsc[par] = [("sV", K.count("sV"))]
                K._snapsc = snapsc
            snapsc = K._snapsc
            vop(lambda e, s: e.tensor_scalar(col(NEGM0), col(MCOL0), -1.0, None, ALU.mult))
            vop(lambda e, s: e.tensor_scalar(col(NEGM1), col(MCOL1), -1.0, None, ALU.mult),
                inc=True)
            K.wait("scalar", "sV")
            K.wait("scalar", "qPT")
            for qt in range(2):
                aop(lambda e, s, qt=qt: e.activation(
                    p_bf[qt][:, :], scores[qt][:, :], AF.Exp,
                    bias=col(NEGM0 if qt == 0 else NEGM1),
                    accum_out=col(LCOL0 if qt == 0 else LCOL1)), inc=True)
            K.wait("vector", "sC")
            vop(lambda e, s: e.reciprocal(col(LINV0), col(LCOL0)))
            vop(lambda e, s: e.reciprocal(col(LINV1), col(LCOL1)), inc=True)
            K.wait("sync", "sC")
            K.wait("sync", "sT")
            for qt in range(2):
                dmat("qPT", pT[:, :, qt * 128:(qt + 1) * 128], p_bf[qt][:, :])
            for kc in range(32):
                par = kc % 4
                xb_t = xnb_tile[par]
                qn = "qXT%d" % par
                if par in snapx:
                    for sem, val in snapx[par]:
                        K.wait("scalar", sem, val)
                dma("scalar", qn, xb_t, d_xnb[kc * 128:(kc + 1) * 128, :])
                K.wait("tensor", qn)
                if kc == 0:
                    K.wait("tensor", "qPT")
                    K.wait("tensor", "sV")
                for qt in range(2):
                    for hbk in range(4):
                        mm(pT[:, kc, qt * 128:(qt + 1) * 128],
                           xb_t[:, hbk * 512:(hbk + 1) * 512],
                           PB[qt * 4 + hbk][:, :], start=(kc == 0), stop=(kc == 31),
                           inc=(qt == 1 and hbk == 3))
                snapx[par] = [("sT", K.count("sT"))]
            K.wait("vector", "sT")
            K.wait("vector", "qPX")
            for qt in range(2):
                for hbk in range(4):
                    vop(lambda e, s, qt=qt, hbk=hbk: e.tensor_scalar(
                        px_scr[:, hbk * 512:(hbk + 1) * 512], PB[qt * 4 + hbk][:, :],
                        col(LINV0 if qt == 0 else LINV1), None, ALU.mult),
                        inc=(hbk == 3))
                K.wait("sync", "sV")
                dmat("qPX", px_fm[:, :, qq * QQ + qt * 128: qq * QQ + (qt + 1) * 128],
                     px_scr[:, :])
                if qt == 0:
                    K.wait("vector", "qPX")
        K.barrier()
        dma("gpsimd", "qWV", wbig[:, :, :],
            wvt.rearrange("(c p) q -> p c q", p=128))
        K.wait("tensor", "qWV")
        bank = 0
        snapc = {}
        for hv in range(HC):
            slot = hv % 4
            if slot in snapc:
                for sem, val in snapc[slot]:
                    K.wait("tensor", sem, val)
            for hc in range(HC):
                for qb in range(2):
                    mm(wbig[:, hc, hv * 128:(hv + 1) * 128],
                       px_fm[:, hc, qb * 512:(qb + 1) * 512],
                       PB[(bank + qb) % 8][:, :],
                       start=(hc == 0), stop=(hc == HC - 1),
                       inc=(hc == HC - 1 and qb == 1))
            K.wait("vector", "sT")
            for qb in range(2):
                vop(lambda e, s, hv=hv, qb=qb, bank=bank: e.tensor_scalar(
                    ctx_fm[:, hv, qb * 512:(qb + 1) * 512],
                    PB[(bank + qb) % 8][:, :], bv_cols[:, hv:hv + 1], None, ALU.add),
                    inc=True)
            snapc[slot] = [("sV", K.count("sV"))]
            bank += 2
        K.barrier()
        dma("gpsimd", "qWO", wbig[:, :, :],
            wot.rearrange("(c p) q -> p c q", p=128))
        K.wait("tensor", "qWO")
        snapo = None
        snapf = {}
        for ob in range(4):
            K.wait("gpsimd", "sV")
            dma("gpsimd", "qBO", bo_bc,
                bov.rearrange("(o n) -> o n", o=1)[0:1, ob * 512:(ob + 1) * 512]
                .broadcast_to([128, 512]))

            def opj_load(t, ob=ob):
                par = t % 4
                if par in snapf:
                    for sem, val in snapf[par]:
                        K.wait("gpsimd", sem, val)
                dma("gpsimd", "qFI%d" % par, gf_tile[par],
                    d_gf[t, :, ob * 512:(ob + 1) * 512])
                dma("gpsimd", "qFI%d" % par, xo_blk[par],
                    xo[t * 128:(t + 1) * 128, ob * 512:(ob + 1) * 512])
            for hv in range(HC):
                if hv == 0 and snapo is not None:
                    K.wait("tensor", "sV", snapo)
                for t in range(TT):
                    mm(ctx_fm[:, hv, t * 128:(t + 1) * 128],
                       wbig[:, hv, ob * 512:(ob + 1) * 512], PB[t][:, :],
                       start=(hv == 0), stop=(hv == HC - 1),
                       inc=(hv == HC - 1 and t == TT - 1))
            K.wait("vector", "sT")
            K.wait("vector", "qBO")
            for t in range(TT):
                vop(lambda e, s, t=t: e.tensor_tensor(s1_big[:, t, :], PB[t][:, :],
                                                      bo_bc[:, :], ALU.add), inc=True)
            snapo = K.count("sV")
            for t in range(4):
                opj_load(t)
            for t in range(TT):
                par = t % 4
                K.wait("vector", "qFI%d" % par)
                K.wait("vector", "qOUT%d" % (t % 2))
                vop(lambda e, s, t=t, par=par: e.tensor_tensor(
                    s1_big[:, t, :], s1_big[:, t, :], gf_tile[par][:, :], ALU.mult))
                vop(lambda e, s, t=t, par=par: e.tensor_tensor(
                    out_scr[t % 2][:, :], s1_big[:, t, :], xo_blk[par][:, :], ALU.add),
                    inc=True)
                K.wait("gpsimd", "sV")
                dma("gpsimd", "qOUT%d" % (t % 2),
                    out[t * 128:(t + 1) * 128, ob * 512:(ob + 1) * 512],
                    out_scr[t % 2][:, :])
                snapf[par] = [("sV", K.count("sV"))]
                if t + 4 < TT:
                    opj_load(t + 4)
        K.wait("gpsimd", "qOUT0")
        K.wait("gpsimd", "qOUT1")
        K.wait("gpsimd", "qGF")
        for _p in range(6):
            K.wait("gpsimd", "qQG%d" % _p)
        K.wait("gpsimd", "qSC")

        sems = {n: stack.enter_context(nc.semaphore(n)) for n in sorted(K.sem_names)}
        block = stack.enter_context(nc.Block())

        @block.sync
        def _(e):
            K.replay("sync", e, sems)

        @block.scalar
        def _(e):
            K.replay("scalar", e, sems)

        @block.vector
        def _(e):
            K.replay("vector", e, sems)

        @block.tensor
        def _(e):
            K.replay("tensor", e, sems)

        @block.gpsimd
        def _(e):
            K.replay("gpsimd", e, sems)

    return nc


_NC_CACHE = [None]


def kernel(**inputs):
    x = np.asarray(inputs["L0_attn_hidden_states"], dtype=np.float32)
    wqkv = np.asarray(inputs["L0_attn_w_qkv"], dtype=np.float32)
    bqkv = np.asarray(inputs["L0_attn_b_qkv"], dtype=np.float32)
    w_out = np.asarray(inputs["L0_attn_w_out"], dtype=np.float32)
    b_out = np.asarray(inputs["L0_attn_b_out"], dtype=np.float32)
    w_gate = np.asarray(inputs["L0_ffn_w_gate"], dtype=np.float32)
    w_up = np.asarray(inputs["L0_ffn_w_up"], dtype=np.float32)
    w_down = np.asarray(inputs["L0_ffn_w_down"], dtype=np.float32)
    dperm = np.ascontiguousarray(
        np.asarray(inputs["L0_ffn_down_reorder_index"]).astype(np.int32))

    wq_a = np.ascontiguousarray(wqkv[0:H])
    wk_a = np.ascontiguousarray(wqkv[H:2 * H])
    wvt_a = np.ascontiguousarray(wqkv[2 * H:3 * H].T)
    wot_a = np.ascontiguousarray(w_out.T)
    bq_a = np.ascontiguousarray(bqkv[0:H])
    bv_a = np.ascontiguousarray(bqkv[2 * H:3 * H])
    bo_a = np.ascontiguousarray(b_out)
    wg_a = np.ascontiguousarray(w_gate)
    wu_a = np.ascontiguousarray(w_up)
    wd_a = np.ascontiguousarray(w_down)

    if _NC_CACHE[0] is None:
        _NC_CACHE[0] = build_program()
    nc = _NC_CACHE[0]

    in_maps = []
    for c in range(8):
        b, qb = c // 4, c % 4
        in_maps.append(dict(
            xb=np.ascontiguousarray(x[b]),
            xo=np.ascontiguousarray(x[b, qb * TO:(qb + 1) * TO]),
            wq=wq_a, wk=wk_a, wvt=wvt_a, wot=wot_a,
            bqv=bq_a, bvv=bv_a, bov=bo_a,
            wg=wg_a, wu=wu_a, wd=wd_a, pidx=dperm,
        ))
    res = run_bass_kernel_spmd(nc, in_maps, core_ids=list(range(8)))
    outp = np.empty((2, S, H), dtype=np.float32)
    for c in range(8):
        b, qb = c // 4, c % 4
        outp[b, qb * TO:(qb + 1) * TO] = np.asarray(res.results[c]["out"])
    return outp
